# version 18
# baseline (speedup 1.0000x reference)
import numpy as np
import concourse.bass as bass
import concourse.mybir as mybir
from concourse.bass_utils import run_bass_kernel_spmd
from contextlib import ExitStack

F32 = mybir.dt.float32
BF16 = mybir.dt.bfloat16
AF = mybir.ActivationFunctionType
ALU = mybir.AluOpType

C = 1024
NCH = 8
NLAYERS = 4
NFC = 32
B = 2
T = 8192
NCORES = 8
DELTA = 256
TP = 2048 + 3 * DELTA // 4
TS = 448
NTAU = TP // TS
ENDS = [TP, TP + 1984, TP + 3968, T]
REALS = [TP, 1984, 1984, 1984]
EPS = 1e-5

MATS = [("Wk", 8, 8), ("Wv", 8, 8), ("Wr", 8, 8), ("Wo", 8, 8),
        ("Wck", 32, 8), ("Wcv", 8, 32), ("Wcr", 8, 8)]
FFN_MATS = [("fWr", 8, 8), ("fWk", 8, 8), ("fWv", 8, 8)]
TILES_PER_LAYER = sum(no * nc for _, no, nc in MATS)
FFN_TILES = sum(no * nc for _, no, nc in FFN_MATS)
NT = NLAYERS * TILES_PER_LAYER + FFN_TILES
NBLK = NT // 8
NGRP = NBLK

CV_LN0W, CV_LN0B = 0, 1


def cv_idx(l, name):
    names = ["ln1_w", "ln1_b", "ln2_w", "ln2_b", "td", "tf",
             "tmk", "tmv", "tmr", "cmk", "cmr"]
    return 2 + 11 * l + names.index(name)


CV_MUR = 2 + 11 * NLAYERS
CV_MUK = CV_MUR + 1
NCV = CV_MUK + 1



def _merge(dst, ev):
    if ev is None:
        return
    kind, src, val = ev
    key = (kind, src)
    if val > dst.get(key, -1):
        dst[key] = val


class Region:

    def __init__(self):
        self.w = {}
        self.r = {}

    def rd_deps(self):
        return [("e" if k[0] == "e" else "q", k[1], v) if False else
                (k[0], k[1], v) for k, v in self.w.items()]

    def wr_deps(self):
        out = {(k[0], k[1]): v for k, v in self.w.items()}
        for k, v in self.r.items():
            if v > out.get(k, -1):
                out[k] = v
        return [(k[0], k[1], v) for k, v in out.items()]

    def did_read(self, ev):
        _merge(self.r, ev)

    def did_write(self, ev):
        _merge(self.w, ev)


class Builder:
    ENGINES = ("pe", "act", "dve", "gps", "sp")

    def __init__(self, nc, es):
        self.nc = nc
        self.es = es
        self.cnt = {e: 0 for e in self.ENGINES}
        self.qcnt = {}
        self.plan = {e: [] for e in self.ENGINES}
        self.sem = {}
        for e in self.ENGINES:
            self.sem[("e", e)] = es.enter_context(nc.semaphore("sem_" + e))
        self.waited = {e: {} for e in self.ENGINES}

    def _mk_waits(self, eng, deps):
        need = {}
        for d in deps:
            _merge(need, d)
        out = []
        w = self.waited[eng]
        for key, val in sorted(need.items()):
            if val > w.get(key, 0):
                w[key] = val
                out.append((self.sem[key], val))
        return out

    def op(self, eng, fn, deps=()):
        waits = self._mk_waits(eng, deps)
        self.cnt[eng] += 1
        ev = ("e", eng, self.cnt[eng])
        self.plan[eng].append(("op", waits, fn, None))
        return ev

    def dma(self, qname, eng, fn, deps=()):
        if qname not in self.qcnt:
            self.qcnt[qname] = 0
            self.sem[("q", qname)] = self.es.enter_context(
                self.nc.semaphore("q_" + qname))
        prev = ("q", qname, self.qcnt[qname]) if self.qcnt[qname] else None
        waits = self._mk_waits(eng, list(deps) + ([prev] if prev else []))
        self.qcnt[qname] += 16
        ev = ("q", qname, self.qcnt[qname])
        self.plan[eng].append(("dma", waits, fn, self.sem[("q", qname)]))
        return ev

    def raw_wait(self, eng, deps):
        waits = self._mk_waits(eng, deps)
        if waits:
            self.plan[eng].append(("wait", waits, None, None))

    def emit(self, block):
        deco = {"pe": block.tensor, "act": block.scalar, "dve": block.vector,
                "gps": block.gpsimd, "sp": block.sync}

        def mk(eng):
            items = self.plan[eng]
            osem = self.sem[("e", eng)]

            def body(q):
                for kind, waits, fn, qsem in items:
                    for sem, val in waits:
                        q.wait_ge(sem, val)
                    if kind == "op":
                        fn(q).then_inc(osem, 1)
                    elif kind == "dma":
                        fn(q).then_inc(qsem, 16)
            return body
        for eng in self.ENGINES:
            deco[eng](mk(eng))


class Banks:
    def __init__(self, nc, es):
        self.t = [es.enter_context(nc.psum_tensor(f"bank{i}", [128, 512], F32))
                  for i in range(8)]
        self.state = [{"w": {}, "r": {}, "seq": -i} for i in range(8)]
        self.clock = 0

    def acquire(self, pin=False):
        free = [k for k in range(8) if not self.state[k].get("pin")]
        i = min(free, key=lambda k: self.state[k]["seq"])
        st = self.state[i]
        st["pin"] = pin
        deps = [(k[0], k[1], v) for k, v in st["w"].items()]
        deps += [(k[0], k[1], v) for k, v in st["r"].items()]
        self.clock += 1
        st["seq"] = self.clock
        st["w"] = {}
        st["r"] = {}
        return i, deps

    def unpin(self, i):
        self.state[i]["pin"] = False

    def wrote(self, i, ev):
        _merge(self.state[i]["w"], ev)

    def read(self, i, ev):
        _merge(self.state[i]["r"], ev)



def build_program(n_layers=NLAYERS, with_prep=True, with_ffn=True,
                  tp=TP, ts=TS, dbg_parts=("attn", "cm")):
    ntau = tp // ts
    assert ntau * ts == tp

    nc = bass.Bass("TRN2", target_bir_lowering=False)

    x_h = nc.dram_tensor("x", [128, NCH, tp], F32, kind="ExternalInput").ap()
    cv_h = nc.dram_tensor("cvec", [128, NCV, NCH], F32,
                          kind="ExternalInput").ap()
    w_h = nc.dram_tensor("wsrc", [NGRP, 128, 1024], F32,
                         kind="ExternalInput").ap()
    out_h = nc.dram_tensor("out", [128, NCH, tp], F32,
                           kind="ExternalOutput").ap()
    wc_h = nc.dram_tensor("wcache", [NBLK, 128, 1024], BF16,
                          kind="Internal").ap()

    es = ExitStack()
    with es:
        def sb(name, shape, dt):
            return es.enter_context(nc.sbuf_tensor(name, shape, dt))

        xres = sb("xres", [128, NCH, tp], F32)
        bpool = sb("bpool", [128, 32 * ts], BF16)
        h = sb("h", [128, NCH, ts + 1], BF16)
        dsq = sb("dsq", [128, NCH, ts], BF16)
        ekf = sb("ekf", [128, NCH, ts], BF16)
        ekvf = sb("ekvf", [128, NCH, ts], BF16)
        snf = sb("snf", [128, NCH, ts + 1], F32)
        sdf = sb("sdf", [128, NCH, ts + 1], F32)
        srf = sb("srf", [128, NCH, ts], BF16)
        rwkv = sb("rwkv", [128, NCH, ts], BF16)
        wstage = sb("wstage", [128, 2, 1024], F32)
        wconv = sb("wconv", [128, 2, 1024], BF16)
        wring = sb("wring", [128, 8, 1024], BF16)
        cvec = sb("cvs", [128, NCV, NCH], F32)
        cvb = sb("cvb", [128, NCV, NCH], BF16)
        lam = sb("lam", [128, NLAYERS, NCH], F32)
        eu = sb("eu", [128, NLAYERS, NCH], F32)
        snst = sb("snst", [128, NCH], F32)
        sdst = sb("sdst", [128, NCH], F32)
        bcol2 = sb("bcol2", [128, 3, NCH], BF16)
        ones1 = sb("ones1", [128, 1], BF16)
        onesr = sb("onesr", [1, 128], F32)
        rows = sb("rows", [1, 4, ts], F32)
        epsr = sb("epsr", [1, 1], F32)

        bld = Builder(nc, es)
        banks = Banks(nc, es)

        XK0, XV0, XR0, XB0 = (0, NCH * ts, 2 * NCH * ts, 3 * NCH * ts)
        REGNAME = {XK0: "xk", XV0: "xv", XR0: "xr", XB0: "xb"}

        def bp(base, c):
            return bpool[:, base + c * ts: base + (c + 1) * ts]

        def bp3(base, n=NCH):
            return bpool[:, base: base + n * ts].rearrange(
                "p (c t) -> p c t", c=n)

        KF_REG = ["xv", "xb", "dsq", "h"]

        def kf_view(fc):
            g, c = fc // 8, fc % 8
            if g == 0:
                return bp(XV0, c)
            if g == 1:
                return bp(XB0, c)
            if g == 2:
                return dsq[:, c, :]
            return h[:, c, 0:ts]

        def fp_u3():
            return snf[:, :, 0:ts]

        R_xres = [Region() for _ in range(ntau)]
        R_bp = {k: Region() for k in ("xk", "xv", "xr", "xb")}
        R_h = Region()
        R_dsq = Region()
        R_ek = Region()
        R_ekv = Region()
        R_sn = Region()
        R_sd = Region()
        R_srf = Region()
        R_rows = Region()
        R_rwkv = Region()
        R_bcol = Region()
        R_state = Region()
        R_lam = Region()

        ev_cv = bld.dma("cv", "sp", lambda q: q.dma_start(cvec[:], cv_h))
        ev_cvb = bld.op("act", lambda q: q.activation(cvb[:], cvec[:],
                                                      AF.Copy), deps=[ev_cv])
        ev_eps = bld.op("gps", lambda q: q.memset(epsr[:], float(EPS)))
        ev_ones = bld.op("gps", lambda q: q.memset(ones1[:], 1.0))
        ev_onesr = bld.op("gps", lambda q: q.memset(onesr[:], 1.0))
        e = bld.op("dve", lambda q: q.memset(snst[:], 0.0))
        e = bld.op("dve", lambda q: q.memset(sdst[:], 0.0), deps=[e])
        R_state.did_write(e)
        e = bld.op("dve", lambda q: q.memset(bcol2[:], 0.0))
        R_bcol.did_write(e)

        ev_lam = None
        for l in range(n_layers):
            e1 = bld.op("act", lambda q, l=l: q.activation(
                lam[:, l, :], cvec[:, cv_idx(l, "td"), :], AF.Exp),
                deps=[ev_cv])
            e2 = bld.op("act", lambda q, l=l: q.activation(
                lam[:, l, :], lam[:, l, :], AF.Exp, scale=-1.0), deps=[e1])
            e3 = bld.op("act", lambda q, l=l: q.activation(
                eu[:, l, :], cvec[:, cv_idx(l, "tf"), :], AF.Exp),
                deps=[ev_cv])
            R_lam.did_write(e2)
            R_lam.did_write(e3)
            ev_lam = e3 if ev_lam is None else ev_lam
        ev_lam_all = R_lam.rd_deps()

        ev_xload = []
        for t in range(ntau):
            e = bld.dma("xl%d" % t, "act", lambda q, t=t: q.dma_start(
                xres[:, :, t * ts:(t + 1) * ts],
                x_h[:, :, t * ts:(t + 1) * ts]))
            ev_xload.append(e)
            R_xres[t].did_write(e)

        conv = {"g": 0, "conv_ev": [None, None], "wb_ev": [None] * NBLK}

        def pump(n):
            for _ in range(n):
                g = conv["g"]
                if g >= NGRP:
                    return
                conv["g"] += 1
                s = g % 2
                deps = [conv["conv_ev"][s]] if conv["conv_ev"][s] else []
                ev_ld = bld.dma("wst%d" % s, "gps", lambda q, g=g, s=s:
                                q.dma_start(wstage[:, s, :], w_h[g]),
                                deps=deps)
                wb_prev = conv["wb_ev"][g - 2] if g >= 2 else None
                cdeps = [ev_ld] + ([wb_prev] if wb_prev else [])
                ev_cvt = bld.op("gps", lambda q, s=s: q.tensor_copy(
                    wconv[:, s, :], wstage[:, s, :]), deps=cdeps)
                conv["conv_ev"][s] = ev_cvt
                ev_wb = bld.dma("wcv%d" % s, "gps", lambda q, g=g, s=s:
                                q.dma_start(wc_h[g], wconv[:, s, :]),
                                deps=[ev_cvt])
                conv["wb_ev"][g] = ev_wb

        wst = {"pos": 0, "end": 0, "issued": 0,
               "slot_consumers": [{} for _ in range(4)], "dma_ev": {}}

        def ws_issue_pair(bp_):
            while conv["wb_ev"][2 * bp_] is None or \
                    conv["wb_ev"][2 * bp_ + 1] is None:
                pump(6)
            slot = wst["issued"] % 4
            wst["issued"] += 1
            deps = [(k[0], k[1], v)
                    for k, v in wst["slot_consumers"][slot].items()]
            wst["slot_consumers"][slot] = {}
            deps.append(conv["wb_ev"][2 * bp_])
            deps.append(conv["wb_ev"][2 * bp_ + 1])
            ev = bld.dma("wr%d" % slot, "sp", lambda q, bp_=bp_, slot=slot:
                         q.dma_start(
                             wring[:, 2 * slot:2 * slot + 2, :],
                             wc_h[2 * bp_:2 * bp_ + 2].transpose([1, 0, 2])),
                         deps=deps)
            wst["dma_ev"][bp_] = (ev, slot)

        def ws_start(tile_base, ntiles):
            wst["pos"] = tile_base
            wst["end"] = tile_base + ntiles
            p0 = tile_base // 16
            for k in range(2):
                if (p0 + k) * 16 < wst["end"]:
                    ws_issue_pair(p0 + k)

        def ws_tile(consumer_ev):
            t = wst["pos"]
            wst["pos"] += 1
            bp_, ti = t // 16, t % 16
            ev, slot = wst["dma_ev"][bp_]
            _merge(wst["slot_consumers"][slot], consumer_ev)
            if ti == 0:
                nb = bp_ + 2
                if nb * 16 < wst["end"]:
                    ws_issue_pair(nb)
            ap = wring[:, 2 * slot + ti // 8, (ti % 8) * 128:
                       (ti % 8) * 128 + 128]
            return ap, ev

        def proj_group(moving_fn, nacc, bank, bank_deps, extra_deps=()):
            ev_future = ("e", "pe", bld.cnt["pe"] + 1)
            tiles = [ws_tile(ev_future) for _ in range(nacc)]
            deps = list(bank_deps) + [d for _, d in tiles] + \
                list(extra_deps)

            def go(q):
                last = None
                for j, (ap, _) in enumerate(tiles):
                    last = q.matmul(banks.t[bank][:, 0:ts], ap, moving_fn(j),
                                    start=(j == 0), stop=(j == nacc - 1))
                return last
            ev = bld.op("pe", go, deps=deps)
            assert ev == ev_future
            banks.wrote(bank, ev)
            return ev

        def ln_stats_src(tau, c, deps):
            t0 = tau * ts
            e_xb = bld.op("act", lambda q, c=c, t0=t0: q.activation(
                bp(XB0, c), xres[:, c, t0:t0 + ts], AF.Copy),
                deps=deps + R_bp["xb"].wr_deps())
            R_xres[tau].did_read(e_xb)
            R_bp["xb"].did_write(e_xb)
            e_xsq = bld.op("act", lambda q, c=c, t0=t0: q.activation(
                dsq[:, c, :], xres[:, c, t0:t0 + ts], AF.Square),
                deps=deps + R_dsq.wr_deps())
            R_xres[tau].did_read(e_xsq)
            R_dsq.did_write(e_xsq)

        def ln_front(tau, dep_extra=(), pre=False):
            t0 = tau * ts
            xt = xres[:, :, t0:t0 + ts]
            Rx = R_xres[tau]
            deps0 = Rx.rd_deps() + list(dep_extra)
            if pre:
                e_xb = ("e", "act", bld.cnt["act"])
                e_xsq = e_xb
            else:
                e_xb = bld.op("act", lambda q: q.activation(
                    bp3(XB0), xt, AF.Copy), deps=deps0 + R_bp["xb"].wr_deps())
                Rx.did_read(e_xb)
                R_bp["xb"].did_write(e_xb)
                e_xsq = bld.op("act", lambda q: q.activation(
                    dsq[:], xt, AF.Square), deps=deps0 + R_dsq.wr_deps())
                Rx.did_read(e_xsq)
                R_dsq.did_write(e_xsq)

            b0, d0 = banks.acquire(pin=True)
            b1, d1 = banks.acquire(pin=True)

            def mm_stats(q, bank, use_dsq):
                last = None
                for c in range(NCH):
                    mv = dsq[:, c, :] if use_dsq else bp(XB0, c)
                    last = q.matmul(banks.t[bank][0:1, 0:ts], ones1[:], mv,
                                    start=(c == 0), stop=(c == NCH - 1))
                return last
            e_s0 = bld.op("pe", lambda q, b0=b0: mm_stats(q, b0, False),
                          deps=d0 + [e_xb, ev_ones] + R_bp["xb"].rd_deps())
            banks.wrote(b0, e_s0)
            R_bp["xb"].did_read(e_s0)
            e_s1 = bld.op("pe", lambda q, b1=b1: mm_stats(q, b1, True),
                          deps=d1 + [e_xsq] + R_dsq.rd_deps())
            banks.wrote(b1, e_s1)
            R_dsq.did_read(e_s1)

            rdeps = R_rows.wr_deps() + R_rows.rd_deps()
            e_rm = bld.op("dve", lambda q: q.tensor_scalar_mul(
                rows[:, 0, :], banks.t[b0][0:1, 0:ts], 1.0 / C),
                deps=rdeps + [e_s0])
            banks.read(b0, e_rm)
            e_rv = bld.op("dve", lambda q: q.tensor_scalar_mul(
                rows[:, 2, :], banks.t[b1][0:1, 0:ts], 1.0 / C),
                deps=rdeps + [e_s1])
            banks.read(b1, e_rv)
            banks.unpin(b0)
            banks.unpin(b1)
            e_msq = bld.op("dve", lambda q: q.tensor_tensor(
                rows[:, 1, :], rows[:, 0, :], rows[:, 0, :], ALU.mult),
                deps=[e_rm])
            e_var = bld.op("dve", lambda q: q.tensor_tensor(
                rows[:, 2, :], rows[:, 2, :], rows[:, 1, :], ALU.subtract),
                deps=[e_rv, e_msq])
            e_sq = bld.op("act", lambda q: q.activation(
                rows[:, 3, :], rows[:, 2, :], AF.Ln, bias=epsr[:]),
                deps=[e_var, ev_eps])
            e_rstd = bld.op("act", lambda q: q.activation(
                rows[:, 1, :], rows[:, 3, :], AF.Exp, scale=-0.5),
                deps=[e_sq])
            e_brow = bld.op("dve", lambda q: q.scalar_tensor_tensor(
                rows[:, 0, :], rows[:, 0, :], -1.0, rows[:, 1, :],
                ALU.mult, ALU.mult), deps=[e_rstd])
            R_rows.did_write(e_brow)

            ba, da = banks.acquire(pin=True)
            bb, db = banks.acquire(pin=True)
            e_bca = bld.op("pe", lambda q, ba=ba: q.matmul(
                banks.t[ba][:, 0:ts], onesr[:], rows[:, 1, :],
                start=True, stop=True), deps=da + [e_rstd, ev_onesr])
            banks.wrote(ba, e_bca)
            R_rows.did_read(e_bca)
            e_bcb = bld.op("pe", lambda q, bb=bb: q.matmul(
                banks.t[bb][:, 0:ts], onesr[:], rows[:, 0, :],
                start=True, stop=True), deps=db + [e_brow, ev_onesr])
            banks.wrote(bb, e_bcb)
            R_rows.did_read(e_bcb)
            return dict(tau=tau, ba=ba, bb=bb, e_bca=e_bca, e_bcb=e_bcb)

        def ln_back(ctx, site, mode, wvec=None, bvec=None, ubuf="sn"):
            tau = ctx["tau"]
            t0 = tau * ts
            xt = xres[:, :, t0:t0 + ts]
            Rx = R_xres[tau]
            ba, bb = ctx["ba"], ctx["bb"]
            fs0 = R_ek if ubuf == "ek" else R_sn
            u = ekf[:] if ubuf == "ek" else fp_u3()
            udeps = fs0.wr_deps()
            ue = []
            for c in range(NCH):
                e1 = bld.op("dve", lambda q, c=c, ba=ba: q.tensor_tensor(
                    u[:, c, :], xt[:, c, :], banks.t[ba][:, 0:ts], ALU.mult),
                    deps=udeps + [ctx["e_bca"]])
                banks.read(ba, e1)
                Rx.did_read(e1)
                e2 = bld.op("dve", lambda q, c=c, bb=bb: q.tensor_tensor(
                    u[:, c, :], u[:, c, :], banks.t[bb][:, 0:ts], ALU.add),
                    deps=[e1, ctx["e_bcb"]])
                banks.read(bb, e2)
                ue.append(e2)
            for e2 in ue:
                fs0.did_write(e2)
            banks.unpin(ba)
            banks.unpin(bb)

            if mode == "plain_x":
                e_o = bld.op("dve", lambda q: q.tensor_copy(xt, u),
                             deps=ue + Rx.wr_deps())
                fs0.did_read(e_o)
                Rx.did_write(e_o)
                return
            if mode == "wb_x":
                wdeps = ue + [ev_cv] + Rx.wr_deps()
                for c in range(NCH):
                    e3 = bld.op("act", lambda q, c=c: q.activation(
                        xt[:, c, :], u[:, c, :], AF.Identity,
                        bias=cvec[:, bvec, c:c + 1],
                        scale=cvec[:, wvec, c:c + 1]), deps=wdeps)
                    fs0.did_read(e3)
                    Rx.did_write(e3)
                return
            e_bc = bld.op("dve", lambda q: q.tensor_copy(
                h[:, :, 0:1], bcol2[:, site:site + 1, :].transpose([0, 2, 1])),
                deps=R_h.wr_deps() + R_h.rd_deps() + R_bcol.rd_deps())
            R_bcol.did_read(e_bc)
            R_h.did_write(e_bc)
            evs = []
            if mode == "plain_h":
                e3 = bld.op("dve", lambda q: q.tensor_copy(
                    h[:, :, 1:ts + 1], u), deps=ue + [e_bc])
                fs0.did_read(e3)
                evs.append(e3)
            else:
                for c in range(NCH):
                    e3 = bld.op("act", lambda q, c=c: q.activation(
                        h[:, c, 1:ts + 1], u[:, c, :], AF.Identity,
                        bias=cvec[:, bvec, c:c + 1],
                        scale=cvec[:, wvec, c:c + 1]),
                        deps=ue + [ev_cv, e_bc])
                    fs0.did_read(e3)
                    evs.append(e3)
            for e3 in evs:
                R_h.did_write(e3)
            e_sv = bld.op("dve", lambda q: q.tensor_copy(
                bcol2[:, site:site + 1, :],
                h[:, :, ts:ts + 1].transpose([0, 2, 1])),
                deps=evs + R_bcol.wr_deps())
            R_bcol.did_write(e_sv)
            R_h.did_read(e_sv)

        def ln_site(tau, site, mode, wvec=None, bvec=None, dep_extra=(),
                    ubuf="sn"):
            ctx = ln_front(tau, dep_extra)
            ln_back(ctx, site, mode, wvec, bvec, ubuf)

        def mix_d():
            e_d = bld.op("dve", lambda q: q.tensor_tensor(
                dsq[:], h[:, :, 1:ts + 1], h[:, :, 0:ts], ALU.subtract),
                deps=R_h.rd_deps() + R_dsq.wr_deps())
            R_h.did_read(e_d)
            R_dsq.did_write(e_d)
            return e_d

        def mix_one(e_d, base, row):
            reg = R_bp[REGNAME[base]]
            for c in range(NCH):
                e3 = bld.op("dve", lambda q, c=c, base=base, row=row:
                            q.scalar_tensor_tensor(
                                bp(base, c), dsq[:, c, :],
                                cvb[:, row, c:c + 1], h[:, c, 0:ts],
                                ALU.mult, ALU.add),
                            deps=[e_d, ev_cvb] + reg.wr_deps())
                R_dsq.did_read(e3)
                R_h.did_read(e3)
                reg.did_write(e3)

        def mixes(pairs):
            e_d = mix_d()
            for base, row in pairs:
                mix_one(e_d, base, row)

        def attn_tau(l, tau):
            t0 = tau * ts
            pump(3)
            for p in range(NCH):
                bk, dk = banks.acquire(pin=True)
                ek_ev = proj_group(lambda j: bp(XK0, j), NCH, bk, dk,
                                   R_bp["xk"].rd_deps())
                R_bp["xk"].did_read(ek_ev)
                e_ek = bld.op("act", lambda q, p=p, bk=bk: q.activation(
                    ekf[:, p, :], banks.t[bk][:, 0:ts], AF.Exp),
                    deps=[ek_ev] + R_ek.wr_deps())
                banks.read(bk, e_ek)
                R_ek.did_write(e_ek)
                e_eku = bld.op("act", lambda q, p=p, bk=bk, l=l: q.activation(
                    bp(XB0, p), banks.t[bk][:, 0:ts], AF.Exp,
                    bias=cvec[:, cv_idx(l, "tf"), p:p + 1]),
                    deps=[ek_ev, ev_cv] + R_bp["xb"].wr_deps())
                banks.read(bk, e_eku)
                banks.unpin(bk)
                R_bp["xb"].did_write(e_eku)
            ekv_evs = []
            for p in range(NCH):
                bv, dv = banks.acquire()
                ev_v = proj_group(lambda j: bp(XV0, j), NCH, bv, dv,
                                  R_bp["xv"].rd_deps())
                R_bp["xv"].did_read(ev_v)
                e_ekv = bld.op("dve", lambda q, p=p, bv=bv: q.tensor_tensor(
                    ekvf[:, p, :], ekf[:, p, :], banks.t[bv][:, 0:ts],
                    ALU.mult), deps=[ev_v] + R_ekv.wr_deps() + R_ek.rd_deps())
                banks.read(bv, e_ekv)
                R_ek.did_read(e_ekv)
                R_ekv.did_write(e_ekv)
                ekv_evs.append(e_ekv)
                e_ekuv = bld.op("dve", lambda q, p=p, bv=bv: q.tensor_tensor(
                    dsq[:, p, :], bp(XB0, p), banks.t[bv][:, 0:ts],
                    ALU.mult), deps=[ev_v] + R_dsq.wr_deps()
                    + R_bp["xb"].rd_deps())
                banks.read(bv, e_ekuv)
                R_bp["xb"].did_read(e_ekuv)
                R_dsq.did_write(e_ekuv)
            e_ci = bld.op("dve", lambda q: q.tensor_copy(
                snf[:, :, 0:1], snst[:].rearrange("p (c o) -> p c o", o=1)),
                deps=R_state.rd_deps() + R_sn.wr_deps())
            R_sn.did_write(e_ci)
            R_state.did_read(e_ci)
            e_cj = bld.op("dve", lambda q: q.tensor_copy(
                sdf[:, :, 0:1], sdst[:].rearrange("p (c o) -> p c o", o=1)),
                deps=[e_ci] + R_sd.wr_deps())
            R_sd.did_write(e_cj)
            R_state.did_read(e_cj)
            sd_evs = []
            for p in range(NCH):
                lam_b = lam[:, l, p:p + 1].broadcast_to([128, ts])
                e_sn = bld.op("dve", lambda q, p=p, lam_b=lam_b:
                              q.tensor_tensor_scan(
                                  snf[:, p, 1:ts + 1], lam_b,
                                  ekvf[:, p, :], snst[:, p:p + 1],
                                  ALU.mult, ALU.add),
                              deps=[ekv_evs[p], e_cj] + ev_lam_all)
                R_sn.did_write(e_sn)
                R_state.did_read(e_sn)
                R_lam.did_read(e_sn)
                R_ekv.did_read(e_sn)
                e_sd = bld.op("dve", lambda q, p=p, lam_b=lam_b:
                              q.tensor_tensor_scan(
                                  sdf[:, p, 1:ts + 1], lam_b,
                                  ekf[:, p, :], sdst[:, p:p + 1],
                                  ALU.mult, ALU.add), deps=[e_sn])
                R_sd.did_write(e_sd)
                R_state.did_read(e_sd)
                R_lam.did_read(e_sd)
                R_ek.did_read(e_sd)
                sd_evs.append(e_sd)
            e_si = bld.op("dve", lambda q: q.tensor_copy(
                snst[:].rearrange("p (c o) -> p c o", o=1),
                snf[:, :, ts:ts + 1]), deps=R_state.wr_deps() + R_sn.rd_deps())
            R_state.did_write(e_si)
            R_sn.did_read(e_si)
            e_sj = bld.op("dve", lambda q: q.tensor_copy(
                sdst[:].rearrange("p (c o) -> p c o", o=1),
                sdf[:, :, ts:ts + 1]), deps=[e_si] + R_sd.rd_deps())
            R_state.did_write(e_sj)
            R_sd.did_read(e_sj)
            for p in range(NCH):
                e_num = bld.op("dve", lambda q, p=p: q.tensor_tensor(
                    snf[:, p, 0:ts], snf[:, p, 0:ts], dsq[:, p, :],
                    ALU.add), deps=[sd_evs[p], e_si] + R_dsq.rd_deps())
                R_dsq.did_read(e_num)
                R_sn.did_write(e_num)
                e_den = bld.op("dve", lambda q, p=p: q.tensor_tensor(
                    sdf[:, p, 0:ts], sdf[:, p, 0:ts], bp(XB0, p),
                    ALU.add), deps=[e_sj] + R_bp["xb"].rd_deps())
                R_bp["xb"].did_read(e_den)
                R_sd.did_write(e_den)
            for p in range(NCH):
                e_l1 = bld.op("act", lambda q, p=p: q.activation(
                    sdf[:, p, 0:ts], sdf[:, p, 0:ts], AF.Ln),
                    deps=R_sd.rd_deps() + R_sd.wr_deps())
                R_sd.did_write(e_l1)
            for p in range(NCH):
                e_rc = bld.op("act", lambda q, p=p: q.activation(
                    sdf[:, p, 0:ts], sdf[:, p, 0:ts], AF.Exp, scale=-1.0),
                    deps=R_sd.rd_deps() + R_sd.wr_deps())
                R_sd.did_write(e_rc)
            for p in range(NCH):
                br, dr = banks.acquire()
                ev_r = proj_group(lambda j: bp(XR0, j), NCH, br, dr,
                                  R_bp["xr"].rd_deps())
                R_bp["xr"].did_read(ev_r)
                e_sr = bld.op("act", lambda q, p=p, br=br: q.activation(
                    srf[:, p, :], banks.t[br][:, 0:ts], AF.Sigmoid),
                    deps=[ev_r] + R_srf.wr_deps())
                banks.read(br, e_sr)
                R_srf.did_write(e_sr)
            e_wkv = bld.op("dve", lambda q: q.tensor_tensor(
                snf[:, :, 0:ts], snf[:, :, 0:ts], sdf[:, :, 0:ts], ALU.mult),
                deps=R_sd.rd_deps() + R_sn.rd_deps() + R_sn.wr_deps())
            R_sn.did_write(e_wkv)
            R_sd.did_read(e_wkv)
            e_rw = bld.op("dve", lambda q: q.tensor_tensor(
                rwkv[:], snf[:, :, 0:ts], srf[:], ALU.mult),
                deps=[e_wkv] + R_srf.rd_deps() + R_rwkv.rd_deps()
                + R_rwkv.wr_deps())
            R_sn.did_read(e_rw)
            R_srf.did_read(e_rw)
            R_rwkv.did_write(e_rw)
            pump(3)
            for cc in range(NCH):
                bo, do = banks.acquire()
                ev_o = proj_group(lambda j: rwkv[:, j, :], NCH, bo, do,
                                  R_rwkv.rd_deps())
                R_rwkv.did_read(ev_o)
                Rx = R_xres[tau]
                e_add = bld.op("dve", lambda q, cc=cc, bo=bo, t0=t0:
                               q.tensor_tensor(
                                   xres[:, cc, t0:t0 + ts],
                                   xres[:, cc, t0:t0 + ts],
                                   banks.t[bo][:, 0:ts], ALU.add),
                               deps=[ev_o] + Rx.wr_deps() + Rx.rd_deps())
                banks.read(bo, e_add)
                Rx.did_write(e_add)
                ln_stats_src(tau, cc, [e_add])
            ctx2 = ln_front(tau, pre=True)
            pump(3)
            return ctx2

        def kf_view(fc):
            g, c = fc // 8, fc % 8
            if g == 0:
                return bp(XR0, c)
            if g == 1:
                return srf[:, c, :]
            if g == 2:
                return ekf[:, c, :]
            return ekvf[:, c, :]

        def kf_reg(fc):
            return [R_bp["xr"], R_srf, R_ek, R_ekv][fc // 8]

        def cm_tau(l, tau, ctx2, nxt):
            t0 = tau * ts
            ln_back(ctx2, 1, "h", cv_idx(l, "ln2_w"), cv_idx(l, "ln2_b"))
            e_d = mix_d()
            mix_one(e_d, XK0, cv_idx(l, "cmk"))
            mix_one(e_d, XV0, cv_idx(l, "cmr"))
            pump(3)
            kf_evs = []
            for fc in range(NFC):
                bk, dk = banks.acquire()
                ev_k = proj_group(lambda j: bp(XK0, j), NCH, bk, dk,
                                  R_bp["xk"].rd_deps())
                R_bp["xk"].did_read(ev_k)
                dstreg = kf_reg(fc)
                e_rl = bld.op("act", lambda q, fc=fc, bk=bk: q.activation(
                    kf_view(fc), banks.t[bk][:, 0:ts], AF.Relu),
                    deps=[ev_k] + dstreg.wr_deps())
                banks.read(bk, e_rl)
                e_kf = bld.op("act", lambda q, fc=fc: q.activation(
                    kf_view(fc), kf_view(fc), AF.Square), deps=[e_rl])
                dstreg.did_write(e_kf)
                kf_evs.append(e_kf)
                if fc == 3 and nxt is not None:
                    if nxt.get("newlayer"):
                        e_z = bld.op("dve", lambda q: q.memset(
                            bcol2[:, 0:1, :], 0.0),
                            deps=R_bcol.wr_deps() + R_bcol.rd_deps())
                        R_bcol.did_write(e_z)
                    nxt["ctx"] = ln_front(nxt["tau"])
            pump(3)
            if nxt is not None:
                ln_back(nxt["ctx"], 0, "h", cv_idx(nxt["l"], "ln1_w"),
                        cv_idx(nxt["l"], "ln1_b"))
                nxt["e_d"] = mix_d()
            crs_evs = []
            for cc in range(NCH):
                bcr, dcr = banks.acquire()
                ev_cr = proj_group(lambda j: bp(XV0, j), NCH, bcr, dcr,
                                   R_bp["xv"].rd_deps())
                R_bp["xv"].did_read(ev_cr)
                e_sig = bld.op("act", lambda q, cc=cc, bcr=bcr:
                               q.activation(rwkv[:, cc, :],
                                            banks.t[bcr][:, 0:ts],
                                            AF.Sigmoid),
                               deps=[ev_cr] + R_rwkv.wr_deps())
                banks.read(bcr, e_sig)
                R_rwkv.did_write(e_sig)
                crs_evs.append(e_sig)
            if nxt is not None:
                mix_one(nxt["e_d"], XK0, cv_idx(nxt["l"], "tmk"))
                mix_one(nxt["e_d"], XV0, cv_idx(nxt["l"], "tmv"))
            pump(3)
            Rx = R_xres[tau]
            for cc in range(NCH):
                bkv, dkv = banks.acquire()
                ev_kv = proj_group(lambda j: kf_view(j), NFC, bkv, dkv,
                                   kf_evs)
                for fc in range(0, NFC, 8):
                    kf_reg(fc).did_read(ev_kv)
                e_t = bld.op("dve", lambda q, cc=cc, bkv=bkv:
                             q.tensor_tensor(sdf[:, cc % 2, 0:ts],
                                             rwkv[:, cc, :],
                                             banks.t[bkv][:, 0:ts], ALU.mult),
                             deps=[crs_evs[cc], ev_kv] + R_sd.wr_deps())
                banks.read(bkv, e_t)
                R_rwkv.did_read(e_t)
                R_sd.did_write(e_t)
                e_add = bld.op("dve", lambda q, cc=cc, t0=t0:
                               q.tensor_tensor(
                                   xres[:, cc, t0:t0 + ts],
                                   xres[:, cc, t0:t0 + ts],
                                   sdf[:, cc % 2, 0:ts], ALU.add),
                               deps=[e_t] + Rx.wr_deps() + Rx.rd_deps())
                R_sd.did_read(e_add)
                Rx.did_write(e_add)
            if nxt is not None:
                mix_one(nxt["e_d"], XR0, cv_idx(nxt["l"], "tmr"))
            pump(3)

        def ffn_tau(tau):
            t0 = tau * ts
            Rx = R_xres[tau]
            e_xi = bld.dma("xin", "act", lambda q, t0=t0: q.dma_start(
                fp_u3(), x_h[:, :, t0:t0 + ts]), deps=R_sn.wr_deps())
            R_sn.did_write(e_xi)
            e_add = bld.op("dve", lambda q, t0=t0: q.tensor_tensor(
                xres[:, :, t0:t0 + ts], xres[:, :, t0:t0 + ts], fp_u3(),
                ALU.add), deps=[e_xi] + Rx.wr_deps() + Rx.rd_deps())
            R_sn.did_read(e_add)
            Rx.did_write(e_add)

            ln_site(tau, 2, "plain_h", ubuf="sn")
            mixes([(XK0, CV_MUK), (XR0, CV_MUR)])
            pump(6)
            for dc in range(NCH):
                br_, dr_ = banks.acquire()
                ev_r = proj_group(lambda j: bp(XR0, j), NCH, br_, dr_,
                                  R_bp["xr"].rd_deps())
                R_bp["xr"].did_read(ev_r)
                e_sig = bld.op("act", lambda q, dc=dc, br_=br_: q.activation(
                    bp(XB0, dc), banks.t[br_][:, 0:ts], AF.Sigmoid),
                    deps=[ev_r] + R_bp["xb"].wr_deps())
                banks.read(br_, e_sig)
                R_bp["xb"].did_write(e_sig)
            for dc in range(NCH):
                bk_, dk_ = banks.acquire()
                ev_k = proj_group(lambda j: bp(XK0, j), NCH, bk_, dk_,
                                  R_bp["xk"].rd_deps())
                R_bp["xk"].did_read(ev_k)
                e_rl = bld.op("act", lambda q, dc=dc, bk_=bk_: q.activation(
                    bp(XV0, dc), banks.t[bk_][:, 0:ts], AF.Relu),
                    deps=[ev_k] + R_bp["xv"].wr_deps())
                banks.read(bk_, e_rl)
                e_kb = bld.op("dve", lambda q, dc=dc, bk_=bk_: q.tensor_tensor(
                    bp(XV0, dc), bp(XV0, dc), banks.t[bk_][:, 0:ts], ALU.mult),
                    deps=[e_rl])
                banks.read(bk_, e_kb)
                R_bp["xv"].did_write(e_kb)
            pump(6)
            for cc in range(NCH):
                slot = cc % 2
                bv_, dv_ = banks.acquire()
                ev_kv = proj_group(lambda j: bp(XV0, j), NCH, bv_, dv_,
                                   R_bp["xv"].rd_deps())
                R_bp["xv"].did_read(ev_kv)
                e_t = bld.op("dve", lambda q, cc=cc, slot=slot, bv_=bv_:
                             q.tensor_tensor(sdf[:, slot, 0:ts],
                                             bp(XB0, cc),
                                             banks.t[bv_][:, 0:ts], ALU.mult),
                             deps=[ev_kv] + R_bp["xb"].rd_deps()
                             + R_sd.wr_deps())
                banks.read(bv_, e_t)
                R_bp["xb"].did_read(e_t)
                R_sd.did_write(e_t)
                e_add2 = bld.op("dve", lambda q, cc=cc, slot=slot, t0=t0:
                                q.tensor_tensor(
                                    xres[:, cc, t0:t0 + ts],
                                    xres[:, cc, t0:t0 + ts],
                                    sdf[:, slot, 0:ts], ALU.add),
                                deps=[e_t] + Rx.wr_deps() + Rx.rd_deps())
                R_sd.did_read(e_add2)
                Rx.did_write(e_add2)

        pump(12)
        if with_prep:
            for tau in range(ntau):
                ln_site(tau, 0, "plain_x", dep_extra=[ev_xload[tau]],
                        ubuf="sn")
                pump(5)
                ln_site(tau, 0, "wb_x", CV_LN0W, CV_LN0B, ubuf="sn")
                pump(5)

        interleave = dbg_parts == ("attn", "cm") and n_layers > 0 and True
        if interleave:
            ln_site(0, 0, "h", cv_idx(0, "ln1_w"), cv_idx(0, "ln1_b"))
            mixes([(XK0, cv_idx(0, "tmk")), (XV0, cv_idx(0, "tmv")),
                   (XR0, cv_idx(0, "tmr"))])
        for l in range(n_layers):
            e0 = bld.op("dve", lambda q: q.memset(snst[:], 0.0),
                        deps=R_state.wr_deps())
            e1 = bld.op("dve", lambda q: q.memset(sdst[:], 0.0), deps=[e0])
            R_state.did_write(e1)
            e2 = bld.op("dve", lambda q: q.memset(bcol2[:, 1:2, :], 0.0),
                        deps=R_bcol.wr_deps() + R_bcol.rd_deps())
            R_bcol.did_write(e2)
            for tau in range(ntau):
                ws_start(l * TILES_PER_LAYER, TILES_PER_LAYER)
                if not interleave:
                    if "attn" in dbg_parts:
                        ln_site(tau, 0, "h", cv_idx(l, "ln1_w"),
                                cv_idx(l, "ln1_b"))
                        mixes([(XK0, cv_idx(l, "tmk")),
                               (XV0, cv_idx(l, "tmv")),
                               (XR0, cv_idx(l, "tmr"))])
                        attn_tau(l, tau)
                    if "cm" in dbg_parts:
                        if "attn" not in dbg_parts:
                            ws_start(l * TILES_PER_LAYER + 256,
                                     TILES_PER_LAYER - 256)
                        ctx2 = ln_front(tau)
                        cm_tau(l, tau, ctx2, None)
                    continue
                ctx2 = attn_tau(l, tau)
                if tau < ntau - 1:
                    nxt = {"l": l, "tau": tau + 1, "newlayer": False}
                elif l < n_layers - 1:
                    nxt = {"l": l + 1, "tau": 0, "newlayer": True}
                else:
                    nxt = None
                cm_tau(l, tau, ctx2, nxt)

        if with_ffn:
            e2 = bld.op("dve", lambda q: q.memset(bcol2[:, 2:3, :], 0.0),
                        deps=R_bcol.wr_deps() + R_bcol.rd_deps())
            R_bcol.did_write(e2)
            for tau in range(ntau):
                ws_start(NLAYERS * TILES_PER_LAYER, FFN_TILES)
                ffn_tau(tau)

        out_deps = []
        for tau in range(ntau):
            Rx = R_xres[tau]
            e3 = bld.dma("out", "gps", lambda q, tau=tau: q.dma_start(
                out_h[:, :, tau * ts:(tau + 1) * ts],
                xres[:, :, tau * ts:(tau + 1) * ts]), deps=Rx.wr_deps())
            Rx.did_read(e3)
            out_deps.append(e3)
        bld.raw_wait("gps", out_deps)

        with nc.Block() as block:
            bld.emit(block)

    return nc



def _pack_weights(inputs):
    order = []
    for l in range(NLAYERS):
        for nm in ("Wk", "Wv", "Wr"):
            for oc in range(8):
                for cc in range(8):
                    order.append((nm, l, oc, cc))
        for nm, no, ncc in (("Wo", 8, 8), ("Wck", 32, 8)):
            for oc in range(no):
                for cc in range(ncc):
                    order.append((nm, l, oc, cc))
        for oc in range(8):
            for cc in range(8):
                order.append(("Wcr", l, oc, cc))
        for oc in range(8):
            for cc in range(32):
                order.append(("Wcv", l, oc, cc))
    for nm, no, ncc in FFN_MATS:
        for oc in range(no):
            for cc in range(ncc):
                order.append((nm, None, oc, cc))
    assert len(order) == NT

    WT = {}
    for l in range(NLAYERS):
        for nm, key in (("Wk", "L0_attn_Wk"), ("Wv", "L0_attn_Wv"),
                        ("Wr", "L0_attn_Wr"), ("Wo", "L0_attn_Wo"),
                        ("Wck", "L0_attn_Wck"), ("Wcv", "L0_attn_Wcv"),
                        ("Wcr", "L0_attn_Wcr")):
            WT[(nm, l)] = np.ascontiguousarray(inputs[key][l].T)
    WT[("fWr", None)] = np.ascontiguousarray(inputs["L0_ffn_W_r"].T)
    WT[("fWk", None)] = np.ascontiguousarray(inputs["L0_ffn_W_k"].T)
    WT[("fWv", None)] = np.ascontiguousarray(inputs["L0_ffn_W_v"].T)

    wsrc = np.empty((NGRP, 128, 1024), dtype=np.float32)
    for i, (nm, l, oc, cc) in enumerate(order):
        tile = WT[(nm, l)][cc * 128:(cc + 1) * 128, oc * 128:(oc + 1) * 128]
        g, ti = i // 8, i % 8
        wsrc[g][:, ti * 128:(ti + 1) * 128] = tile
    return wsrc


def _pack_cvec(inputs):
    V = np.zeros((NCV, C), dtype=np.float32)
    V[CV_LN0W] = inputs["L0_attn_ln0_w"]
    V[CV_LN0B] = inputs["L0_attn_ln0_b"]
    for l in range(NLAYERS):
        V[cv_idx(l, "ln1_w")] = inputs["L0_attn_ln1_w"][l]
        V[cv_idx(l, "ln1_b")] = inputs["L0_attn_ln1_b"][l]
        V[cv_idx(l, "ln2_w")] = inputs["L0_attn_ln2_w"][l]
        V[cv_idx(l, "ln2_b")] = inputs["L0_attn_ln2_b"][l]
        V[cv_idx(l, "td")] = inputs["L0_attn_time_decay"][l]
        V[cv_idx(l, "tf")] = inputs["L0_attn_time_first"][l]
        V[cv_idx(l, "tmk")] = inputs["L0_attn_tmk"][l]
        V[cv_idx(l, "tmv")] = inputs["L0_attn_tmv"][l]
        V[cv_idx(l, "tmr")] = inputs["L0_attn_tmr"][l]
        V[cv_idx(l, "cmk")] = inputs["L0_attn_cmk"][l]
        V[cv_idx(l, "cmr")] = inputs["L0_attn_cmr"][l]
    V[CV_MUR] = inputs["L0_ffn_mu_r"]
    V[CV_MUK] = inputs["L0_ffn_mu_k"]
    return np.ascontiguousarray(V.reshape(NCV, NCH, 128).transpose(2, 0, 1))


def kernel(**inputs):
    inputs = {k: np.asarray(v) for k, v in inputs.items()}
    x = inputs["L0_attn_x"]
    wsrc = _pack_weights(inputs)
    cvec = _pack_cvec(inputs)

    in_maps = []
    for g in range(NCORES):
        b, j = g // 4, g % 4
        e = ENDS[j]
        s = e - TP
        xc = np.ascontiguousarray(
            x[b, s:e, :].T.reshape(NCH, 128, TP).transpose(1, 0, 2))
        in_maps.append({"x": xc, "cvec": cvec, "wsrc": wsrc})

    nc = build_program()
    res = run_bass_kernel_spmd(nc, in_maps, core_ids=list(range(NCORES)))

    out = np.empty((B, T, C), dtype=np.float32)
    for g in range(NCORES):
        b, j = g // 4, g % 4
        e = ENDS[j]
        o = np.asarray(res.results[g]["out"])
        oc = o.transpose(1, 0, 2).reshape(C, TP).T
        real = REALS[j]
        out[b, e - real:e, :] = oc[TP - real:, :]
    return out


# version 19
# speedup vs baseline: 1.0553x; 1.0553x over previous
import numpy as np
import concourse.bass as bass
import concourse.mybir as mybir
from concourse.bass_utils import run_bass_kernel_spmd
from contextlib import ExitStack

F32 = mybir.dt.float32
BF16 = mybir.dt.bfloat16
AF = mybir.ActivationFunctionType
ALU = mybir.AluOpType

C = 1024
NCH = 8
NLAYERS = 4
NFC = 32
B = 2
T = 8192
NCORES = 8
DELTA = 256
TP = 2048 + 3 * DELTA // 4
TS = 448
NTAU = TP // TS
ENDS = [TP, TP + 1984, TP + 3968, T]
REALS = [TP, 1984, 1984, 1984]
EPS = 1e-5

MATS = [("Wk", 8, 8), ("Wv", 8, 8), ("Wr", 8, 8), ("Wo", 8, 8),
        ("Wck", 32, 8), ("Wcv", 8, 32), ("Wcr", 8, 8)]
FFN_MATS = [("fWr", 8, 8), ("fWk", 8, 8), ("fWv", 8, 8)]
TILES_PER_LAYER = sum(no * nc for _, no, nc in MATS)
FFN_TILES = sum(no * nc for _, no, nc in FFN_MATS)
NT = NLAYERS * TILES_PER_LAYER + FFN_TILES
NBLK = NT // 8
NGRP = NBLK

CV_LN0W, CV_LN0B = 0, 1


def cv_idx(l, name):
    names = ["ln1_w", "ln1_b", "ln2_w", "ln2_b", "td", "tf",
             "tmk", "tmv", "tmr", "cmk", "cmr"]
    return 2 + 11 * l + names.index(name)


CV_MUR = 2 + 11 * NLAYERS
CV_MUK = CV_MUR + 1
NCV = CV_MUK + 1



def _merge(dst, ev):
    if ev is None:
        return
    kind, src, val = ev
    key = (kind, src)
    if val > dst.get(key, -1):
        dst[key] = val


class Region:

    def __init__(self):
        self.w = {}
        self.r = {}

    def rd_deps(self):
        return [("e" if k[0] == "e" else "q", k[1], v) if False else
                (k[0], k[1], v) for k, v in self.w.items()]

    def wr_deps(self):
        out = {(k[0], k[1]): v for k, v in self.w.items()}
        for k, v in self.r.items():
            if v > out.get(k, -1):
                out[k] = v
        return [(k[0], k[1], v) for k, v in out.items()]

    def did_read(self, ev):
        _merge(self.r, ev)

    def did_write(self, ev):
        _merge(self.w, ev)


class Builder:
    ENGINES = ("pe", "act", "dve", "gps", "sp")

    def __init__(self, nc, es):
        self.nc = nc
        self.es = es
        self.cnt = {e: 0 for e in self.ENGINES}
        self.qcnt = {}
        self.plan = {e: [] for e in self.ENGINES}
        self.sem = {}
        for e in self.ENGINES:
            self.sem[("e", e)] = es.enter_context(nc.semaphore("sem_" + e))
        self.waited = {e: {} for e in self.ENGINES}

    def _mk_waits(self, eng, deps):
        need = {}
        for d in deps:
            _merge(need, d)
        out = []
        w = self.waited[eng]
        for key, val in sorted(need.items()):
            if val > w.get(key, 0):
                w[key] = val
                out.append((self.sem[key], val))
        return out

    def op(self, eng, fn, deps=()):
        waits = self._mk_waits(eng, deps)
        self.cnt[eng] += 1
        ev = ("e", eng, self.cnt[eng])
        self.plan[eng].append(("op", waits, fn, None))
        return ev

    def dma(self, qname, eng, fn, deps=()):
        if qname not in self.qcnt:
            self.qcnt[qname] = 0
            self.sem[("q", qname)] = self.es.enter_context(
                self.nc.semaphore("q_" + qname))
        prev = ("q", qname, self.qcnt[qname]) if self.qcnt[qname] else None
        waits = self._mk_waits(eng, list(deps) + ([prev] if prev else []))
        self.qcnt[qname] += 16
        ev = ("q", qname, self.qcnt[qname])
        self.plan[eng].append(("dma", waits, fn, self.sem[("q", qname)]))
        return ev

    def raw_wait(self, eng, deps):
        waits = self._mk_waits(eng, deps)
        if waits:
            self.plan[eng].append(("wait", waits, None, None))

    def emit(self, block):
        deco = {"pe": block.tensor, "act": block.scalar, "dve": block.vector,
                "gps": block.gpsimd, "sp": block.sync}

        def mk(eng):
            items = self.plan[eng]
            osem = self.sem[("e", eng)]

            def body(q):
                for kind, waits, fn, qsem in items:
                    for sem, val in waits:
                        q.wait_ge(sem, val)
                    if kind == "op":
                        fn(q).then_inc(osem, 1)
                    elif kind == "dma":
                        fn(q).then_inc(qsem, 16)
            return body
        for eng in self.ENGINES:
            deco[eng](mk(eng))


class Banks:
    def __init__(self, nc, es):
        self.t = [es.enter_context(nc.psum_tensor(f"bank{i}", [128, 512], F32))
                  for i in range(8)]
        self.state = [{"w": {}, "r": {}, "seq": -i} for i in range(8)]
        self.clock = 0

    def acquire(self, pin=False):
        free = [k for k in range(8) if not self.state[k].get("pin")]
        i = min(free, key=lambda k: self.state[k]["seq"])
        st = self.state[i]
        st["pin"] = pin
        deps = [(k[0], k[1], v) for k, v in st["w"].items()]
        deps += [(k[0], k[1], v) for k, v in st["r"].items()]
        self.clock += 1
        st["seq"] = self.clock
        st["w"] = {}
        st["r"] = {}
        return i, deps

    def unpin(self, i):
        self.state[i]["pin"] = False

    def wrote(self, i, ev):
        _merge(self.state[i]["w"], ev)

    def read(self, i, ev):
        _merge(self.state[i]["r"], ev)



def build_program(n_layers=NLAYERS, with_prep=True, with_ffn=True,
                  tp=TP, ts=TS, dbg_parts=("attn", "cm")):
    ntau = tp // ts
    assert ntau * ts == tp

    nc = bass.Bass("TRN2", target_bir_lowering=False)

    x_h = nc.dram_tensor("x", [128, NCH, tp], F32, kind="ExternalInput").ap()
    cv_h = nc.dram_tensor("cvec", [128, NCV, NCH], F32,
                          kind="ExternalInput").ap()
    w_h = nc.dram_tensor("wsrc", [NGRP, 128, 1024], F32,
                         kind="ExternalInput").ap()
    out_h = nc.dram_tensor("out", [128, NCH, tp], F32,
                           kind="ExternalOutput").ap()
    wc_h = nc.dram_tensor("wcache", [NBLK, 128, 1024], BF16,
                          kind="Internal").ap()

    es = ExitStack()
    with es:
        def sb(name, shape, dt):
            return es.enter_context(nc.sbuf_tensor(name, shape, dt))

        xres = sb("xres", [128, NCH, tp], F32)
        bpool = sb("bpool", [128, 32 * ts], BF16)
        h = sb("h", [128, NCH, ts + 1], BF16)
        dsq = sb("dsq", [128, NCH, ts], BF16)
        ekf = sb("ekf", [128, NCH, ts], BF16)
        ekvf = sb("ekvf", [128, NCH, ts], BF16)
        snf = sb("snf", [128, NCH, ts + 1], F32)
        sdf = sb("sdf", [128, NCH, ts + 1], F32)
        srf = sb("srf", [128, NCH, ts], BF16)
        rwkv = sb("rwkv", [128, NCH, ts], BF16)
        wstage = sb("wstage", [128, 2, 1024], F32)
        wconv = sb("wconv", [128, 2, 1024], BF16)
        wring = sb("wring", [128, 8, 1024], BF16)
        cvec = sb("cvs", [128, NCV, NCH], F32)
        cvb = sb("cvb", [128, NCV, NCH], BF16)
        lam = sb("lam", [128, NLAYERS, NCH], F32)
        eu = sb("eu", [128, NLAYERS, NCH], F32)
        snst = sb("snst", [128, NCH], F32)
        sdst = sb("sdst", [128, NCH], F32)
        bcol2 = sb("bcol2", [128, 3, NCH], BF16)
        ones1 = sb("ones1", [128, 1], BF16)
        onesr = sb("onesr", [1, 128], F32)
        rows = sb("rows", [1, 4, ts], F32)
        epsr = sb("epsr", [1, 1], F32)

        bld = Builder(nc, es)
        banks = Banks(nc, es)

        XK0, XV0, XR0, XB0 = (0, NCH * ts, 2 * NCH * ts, 3 * NCH * ts)
        REGNAME = {XK0: "xk", XV0: "xv", XR0: "xr", XB0: "xb"}

        def bp(base, c):
            return bpool[:, base + c * ts: base + (c + 1) * ts]

        def bp3(base, n=NCH):
            return bpool[:, base: base + n * ts].rearrange(
                "p (c t) -> p c t", c=n)

        KF_REG = ["xv", "xb", "dsq", "h"]

        def kf_view(fc):
            g, c = fc // 8, fc % 8
            if g == 0:
                return bp(XV0, c)
            if g == 1:
                return bp(XB0, c)
            if g == 2:
                return dsq[:, c, :]
            return h[:, c, 0:ts]

        def fp_u3():
            return snf[:, :, 0:ts]

        R_xres = [Region() for _ in range(ntau)]
        R_bp = {k: Region() for k in ("xk", "xv", "xr", "xb")}
        R_h = Region()
        R_dsq = Region()
        R_ek = Region()
        R_ekv = Region()
        R_sn = Region()
        R_sd = Region()
        R_srf = Region()
        R_rows = Region()
        R_rwkv = Region()
        R_bcol = Region()
        R_state = Region()
        R_lam = Region()

        ev_cv = bld.dma("cv", "sp", lambda q: q.dma_start(cvec[:], cv_h))
        ev_cvb = bld.op("act", lambda q: q.activation(cvb[:], cvec[:],
                                                      AF.Copy), deps=[ev_cv])
        ev_eps = bld.op("gps", lambda q: q.memset(epsr[:], float(EPS)))
        ev_ones = bld.op("gps", lambda q: q.memset(ones1[:], 1.0))
        ev_onesr = bld.op("gps", lambda q: q.memset(onesr[:], 1.0))
        e = bld.op("dve", lambda q: q.memset(snst[:], 0.0))
        e = bld.op("dve", lambda q: q.memset(sdst[:], 0.0), deps=[e])
        R_state.did_write(e)
        e = bld.op("dve", lambda q: q.memset(bcol2[:], 0.0))
        R_bcol.did_write(e)

        ev_lam = None
        for l in range(n_layers):
            e1 = bld.op("act", lambda q, l=l: q.activation(
                lam[:, l, :], cvec[:, cv_idx(l, "td"), :], AF.Exp),
                deps=[ev_cv])
            e2 = bld.op("act", lambda q, l=l: q.activation(
                lam[:, l, :], lam[:, l, :], AF.Exp, scale=-1.0), deps=[e1])
            e3 = bld.op("act", lambda q, l=l: q.activation(
                eu[:, l, :], cvec[:, cv_idx(l, "tf"), :], AF.Exp),
                deps=[ev_cv])
            R_lam.did_write(e2)
            R_lam.did_write(e3)
            ev_lam = e3 if ev_lam is None else ev_lam
        ev_lam_all = R_lam.rd_deps()

        ev_xload = []
        for t in range(ntau):
            e = bld.dma("xl%d" % t, "act", lambda q, t=t: q.dma_start(
                xres[:, :, t * ts:(t + 1) * ts],
                x_h[:, :, t * ts:(t + 1) * ts]))
            ev_xload.append(e)
            R_xres[t].did_write(e)

        conv = {"g": 0, "conv_ev": [None, None], "wb_ev": [None] * NBLK}

        def pump(n):
            for _ in range(n):
                g = conv["g"]
                if g >= NGRP:
                    return
                conv["g"] += 1
                s = g % 2
                deps = [conv["conv_ev"][s]] if conv["conv_ev"][s] else []
                ev_ld = bld.dma("wst%d" % s, "gps", lambda q, g=g, s=s:
                                q.dma_start(wstage[:, s, :], w_h[g]),
                                deps=deps)
                wb_prev = conv["wb_ev"][g - 2] if g >= 2 else None
                cdeps = [ev_ld] + ([wb_prev] if wb_prev else [])
                ev_cvt = bld.op("gps", lambda q, s=s: q.tensor_copy(
                    wconv[:, s, :], wstage[:, s, :]), deps=cdeps)
                conv["conv_ev"][s] = ev_cvt
                ev_wb = bld.dma("wcv%d" % s, "gps", lambda q, g=g, s=s:
                                q.dma_start(wc_h[g], wconv[:, s, :]),
                                deps=[ev_cvt])
                conv["wb_ev"][g] = ev_wb

        wst = {"pos": 0, "end": 0, "issued": 0,
               "slot_consumers": [{} for _ in range(4)], "dma_ev": {}}

        def ws_issue_pair(bp_):
            while conv["wb_ev"][2 * bp_] is None or \
                    conv["wb_ev"][2 * bp_ + 1] is None:
                pump(6)
            slot = wst["issued"] % 4
            wst["issued"] += 1
            deps = [(k[0], k[1], v)
                    for k, v in wst["slot_consumers"][slot].items()]
            wst["slot_consumers"][slot] = {}
            deps.append(conv["wb_ev"][2 * bp_])
            deps.append(conv["wb_ev"][2 * bp_ + 1])
            ev = bld.dma("wr%d" % slot, "sp", lambda q, bp_=bp_, slot=slot:
                         q.dma_start(
                             wring[:, 2 * slot:2 * slot + 2, :],
                             wc_h[2 * bp_:2 * bp_ + 2].transpose([1, 0, 2])),
                         deps=deps)
            wst["dma_ev"][bp_] = (ev, slot)

        def ws_start(tile_base, ntiles):
            wst["pos"] = tile_base
            wst["end"] = tile_base + ntiles
            p0 = tile_base // 16
            for k in range(2):
                if (p0 + k) * 16 < wst["end"]:
                    ws_issue_pair(p0 + k)

        def ws_tile(consumer_ev):
            t = wst["pos"]
            wst["pos"] += 1
            bp_, ti = t // 16, t % 16
            ev, slot = wst["dma_ev"][bp_]
            _merge(wst["slot_consumers"][slot], consumer_ev)
            if ti == 0:
                nb = bp_ + 2
                if nb * 16 < wst["end"]:
                    ws_issue_pair(nb)
            ap = wring[:, 2 * slot + ti // 8, (ti % 8) * 128:
                       (ti % 8) * 128 + 128]
            return ap, ev

        def proj_group(moving_fn, nacc, bank, bank_deps, extra_deps=()):
            ev_future = ("e", "pe", bld.cnt["pe"] + 1)
            tiles = [ws_tile(ev_future) for _ in range(nacc)]
            deps = list(bank_deps) + [d for _, d in tiles] + \
                list(extra_deps)

            def go(q):
                last = None
                for j, (ap, _) in enumerate(tiles):
                    last = q.matmul(banks.t[bank][:, 0:ts], ap, moving_fn(j),
                                    start=(j == 0), stop=(j == nacc - 1))
                return last
            ev = bld.op("pe", go, deps=deps)
            assert ev == ev_future
            banks.wrote(bank, ev)
            return ev

        def ln_stats_src(tau, c, deps):
            t0 = tau * ts
            e_xb = bld.op("act", lambda q, c=c, t0=t0: q.activation(
                bp(XB0, c), xres[:, c, t0:t0 + ts], AF.Copy),
                deps=deps + R_bp["xb"].wr_deps())
            R_xres[tau].did_read(e_xb)
            R_bp["xb"].did_write(e_xb)
            e_xsq = bld.op("act", lambda q, c=c, t0=t0: q.activation(
                dsq[:, c, :], xres[:, c, t0:t0 + ts], AF.Square),
                deps=deps + R_dsq.wr_deps())
            R_xres[tau].did_read(e_xsq)
            R_dsq.did_write(e_xsq)

        def ln_front(tau, dep_extra=(), pre=False):
            t0 = tau * ts
            xt = xres[:, :, t0:t0 + ts]
            Rx = R_xres[tau]
            deps0 = Rx.rd_deps() + list(dep_extra)
            if pre:
                e_xb = ("e", "act", bld.cnt["act"])
                e_xsq = e_xb
            else:
                e_xb = bld.op("act", lambda q: q.activation(
                    bp3(XB0), xt, AF.Copy), deps=deps0 + R_bp["xb"].wr_deps())
                Rx.did_read(e_xb)
                R_bp["xb"].did_write(e_xb)
                e_xsq = bld.op("act", lambda q: q.activation(
                    dsq[:], xt, AF.Square), deps=deps0 + R_dsq.wr_deps())
                Rx.did_read(e_xsq)
                R_dsq.did_write(e_xsq)

            b0, d0 = banks.acquire(pin=True)
            b1, d1 = banks.acquire(pin=True)

            def mm_stats(q, bank, use_dsq):
                last = None
                for c in range(NCH):
                    mv = dsq[:, c, :] if use_dsq else bp(XB0, c)
                    last = q.matmul(banks.t[bank][0:1, 0:ts], ones1[:], mv,
                                    start=(c == 0), stop=(c == NCH - 1))
                return last
            e_s0 = bld.op("pe", lambda q, b0=b0: mm_stats(q, b0, False),
                          deps=d0 + [e_xb, ev_ones] + R_bp["xb"].rd_deps())
            banks.wrote(b0, e_s0)
            R_bp["xb"].did_read(e_s0)
            e_s1 = bld.op("pe", lambda q, b1=b1: mm_stats(q, b1, True),
                          deps=d1 + [e_xsq] + R_dsq.rd_deps())
            banks.wrote(b1, e_s1)
            R_dsq.did_read(e_s1)

            rdeps = R_rows.wr_deps() + R_rows.rd_deps()
            e_rm = bld.op("dve", lambda q: q.tensor_scalar_mul(
                rows[:, 0, :], banks.t[b0][0:1, 0:ts], 1.0 / C),
                deps=rdeps + [e_s0])
            banks.read(b0, e_rm)
            e_rv = bld.op("dve", lambda q: q.tensor_scalar_mul(
                rows[:, 2, :], banks.t[b1][0:1, 0:ts], 1.0 / C),
                deps=rdeps + [e_s1])
            banks.read(b1, e_rv)
            banks.unpin(b0)
            banks.unpin(b1)
            e_msq = bld.op("dve", lambda q: q.tensor_tensor(
                rows[:, 1, :], rows[:, 0, :], rows[:, 0, :], ALU.mult),
                deps=[e_rm])
            e_var = bld.op("dve", lambda q: q.tensor_tensor(
                rows[:, 2, :], rows[:, 2, :], rows[:, 1, :], ALU.subtract),
                deps=[e_rv, e_msq])
            e_sq = bld.op("act", lambda q: q.activation(
                rows[:, 3, :], rows[:, 2, :], AF.Ln, bias=epsr[:]),
                deps=[e_var, ev_eps])
            e_rstd = bld.op("act", lambda q: q.activation(
                rows[:, 1, :], rows[:, 3, :], AF.Exp, scale=-0.5),
                deps=[e_sq])
            e_brow = bld.op("dve", lambda q: q.scalar_tensor_tensor(
                rows[:, 0, :], rows[:, 0, :], -1.0, rows[:, 1, :],
                ALU.mult, ALU.mult), deps=[e_rstd])
            R_rows.did_write(e_brow)

            ba, da = banks.acquire(pin=True)
            bb, db = banks.acquire(pin=True)
            e_bca = bld.op("pe", lambda q, ba=ba: q.matmul(
                banks.t[ba][:, 0:ts], onesr[:], rows[:, 1, :],
                start=True, stop=True), deps=da + [e_rstd, ev_onesr])
            banks.wrote(ba, e_bca)
            R_rows.did_read(e_bca)
            e_bcb = bld.op("pe", lambda q, bb=bb: q.matmul(
                banks.t[bb][:, 0:ts], onesr[:], rows[:, 0, :],
                start=True, stop=True), deps=db + [e_brow, ev_onesr])
            banks.wrote(bb, e_bcb)
            R_rows.did_read(e_bcb)
            return dict(tau=tau, ba=ba, bb=bb, e_bca=e_bca, e_bcb=e_bcb)

        def ln_back(ctx, site, mode, wvec=None, bvec=None, ubuf="sn"):
            tau = ctx["tau"]
            t0 = tau * ts
            xt = xres[:, :, t0:t0 + ts]
            Rx = R_xres[tau]
            ba, bb = ctx["ba"], ctx["bb"]
            fs0 = R_ek if ubuf == "ek" else R_sn
            u = ekf[:] if ubuf == "ek" else fp_u3()
            udeps = fs0.wr_deps()
            ue = []
            for c in range(NCH):
                e1 = bld.op("dve", lambda q, c=c, ba=ba: q.tensor_tensor(
                    u[:, c, :], xt[:, c, :], banks.t[ba][:, 0:ts], ALU.mult),
                    deps=udeps + [ctx["e_bca"]])
                banks.read(ba, e1)
                Rx.did_read(e1)
                e2 = bld.op("dve", lambda q, c=c, bb=bb: q.tensor_tensor(
                    u[:, c, :], u[:, c, :], banks.t[bb][:, 0:ts], ALU.add),
                    deps=[e1, ctx["e_bcb"]])
                banks.read(bb, e2)
                ue.append(e2)
            for e2 in ue:
                fs0.did_write(e2)
            banks.unpin(ba)
            banks.unpin(bb)

            if mode == "plain_x":
                e_o = bld.op("dve", lambda q: q.tensor_copy(xt, u),
                             deps=ue + Rx.wr_deps())
                fs0.did_read(e_o)
                Rx.did_write(e_o)
                return
            if mode == "wb_x":
                wdeps = ue + [ev_cv] + Rx.wr_deps()
                for c in range(NCH):
                    e3 = bld.op("act", lambda q, c=c: q.activation(
                        xt[:, c, :], u[:, c, :], AF.Identity,
                        bias=cvec[:, bvec, c:c + 1],
                        scale=cvec[:, wvec, c:c + 1]), deps=wdeps)
                    fs0.did_read(e3)
                    Rx.did_write(e3)
                return
            e_bc = bld.op("dve", lambda q: q.tensor_copy(
                h[:, :, 0:1], bcol2[:, site:site + 1, :].transpose([0, 2, 1])),
                deps=R_h.wr_deps() + R_h.rd_deps() + R_bcol.rd_deps())
            R_bcol.did_read(e_bc)
            R_h.did_write(e_bc)
            evs = []
            if mode == "plain_h":
                e3 = bld.op("dve", lambda q: q.tensor_copy(
                    h[:, :, 1:ts + 1], u), deps=ue + [e_bc])
                fs0.did_read(e3)
                evs.append(e3)
            else:
                for c in range(NCH):
                    e3 = bld.op("act", lambda q, c=c: q.activation(
                        h[:, c, 1:ts + 1], u[:, c, :], AF.Identity,
                        bias=cvec[:, bvec, c:c + 1],
                        scale=cvec[:, wvec, c:c + 1]),
                        deps=ue + [ev_cv, e_bc])
                    fs0.did_read(e3)
                    evs.append(e3)
            for e3 in evs:
                R_h.did_write(e3)
            e_sv = bld.op("dve", lambda q: q.tensor_copy(
                bcol2[:, site:site + 1, :],
                h[:, :, ts:ts + 1].transpose([0, 2, 1])),
                deps=evs + R_bcol.wr_deps())
            R_bcol.did_write(e_sv)
            R_h.did_read(e_sv)

        def ln_site(tau, site, mode, wvec=None, bvec=None, dep_extra=(),
                    ubuf="sn"):
            ctx = ln_front(tau, dep_extra)
            ln_back(ctx, site, mode, wvec, bvec, ubuf)

        def mix_d():
            e_d = bld.op("dve", lambda q: q.tensor_tensor(
                dsq[:], h[:, :, 1:ts + 1], h[:, :, 0:ts], ALU.subtract),
                deps=R_h.rd_deps() + R_dsq.wr_deps())
            R_h.did_read(e_d)
            R_dsq.did_write(e_d)
            return e_d

        def mix_one(e_d, base, row):
            reg = R_bp[REGNAME[base]]
            for c in range(NCH):
                e3 = bld.op("dve", lambda q, c=c, base=base, row=row:
                            q.scalar_tensor_tensor(
                                bp(base, c), dsq[:, c, :],
                                cvb[:, row, c:c + 1], h[:, c, 0:ts],
                                ALU.mult, ALU.add),
                            deps=[e_d, ev_cvb] + reg.wr_deps())
                R_dsq.did_read(e3)
                R_h.did_read(e3)
                reg.did_write(e3)

        def mixes(pairs):
            e_d = mix_d()
            for base, row in pairs:
                mix_one(e_d, base, row)

        def attn_tau(l, tau):
            t0 = tau * ts
            pump(3)
            for p in range(NCH):
                bk, dk = banks.acquire(pin=True)
                ek_ev = proj_group(lambda j: bp(XK0, j), NCH, bk, dk,
                                   R_bp["xk"].rd_deps())
                R_bp["xk"].did_read(ek_ev)
                e_ek = bld.op("act", lambda q, p=p, bk=bk: q.activation(
                    ekf[:, p, :], banks.t[bk][:, 0:ts], AF.Exp),
                    deps=[ek_ev] + R_ek.wr_deps())
                banks.read(bk, e_ek)
                R_ek.did_write(e_ek)
                e_eku = bld.op("act", lambda q, p=p, bk=bk, l=l: q.activation(
                    bp(XB0, p), banks.t[bk][:, 0:ts], AF.Exp,
                    bias=cvec[:, cv_idx(l, "tf"), p:p + 1]),
                    deps=[ek_ev, ev_cv] + R_bp["xb"].wr_deps())
                banks.read(bk, e_eku)
                banks.unpin(bk)
                R_bp["xb"].did_write(e_eku)
            ekv_evs = []
            for p in range(NCH):
                bv, dv = banks.acquire()
                ev_v = proj_group(lambda j: bp(XV0, j), NCH, bv, dv,
                                  R_bp["xv"].rd_deps())
                R_bp["xv"].did_read(ev_v)
                e_ekv = bld.op("dve", lambda q, p=p, bv=bv: q.tensor_tensor(
                    ekvf[:, p, :], ekf[:, p, :], banks.t[bv][:, 0:ts],
                    ALU.mult), deps=[ev_v] + R_ekv.wr_deps() + R_ek.rd_deps())
                banks.read(bv, e_ekv)
                R_ek.did_read(e_ekv)
                R_ekv.did_write(e_ekv)
                ekv_evs.append(e_ekv)
                e_ekuv = bld.op("dve", lambda q, p=p, bv=bv: q.tensor_tensor(
                    dsq[:, p, :], bp(XB0, p), banks.t[bv][:, 0:ts],
                    ALU.mult), deps=[ev_v] + R_dsq.wr_deps()
                    + R_bp["xb"].rd_deps())
                banks.read(bv, e_ekuv)
                R_bp["xb"].did_read(e_ekuv)
                R_dsq.did_write(e_ekuv)
            e_ci = bld.op("dve", lambda q: q.tensor_copy(
                snf[:, :, 0:1], snst[:].rearrange("p (c o) -> p c o", o=1)),
                deps=R_state.rd_deps() + R_sn.wr_deps())
            R_sn.did_write(e_ci)
            R_state.did_read(e_ci)
            e_cj = bld.op("dve", lambda q: q.tensor_copy(
                sdf[:, :, 0:1], sdst[:].rearrange("p (c o) -> p c o", o=1)),
                deps=[e_ci] + R_sd.wr_deps())
            R_sd.did_write(e_cj)
            R_state.did_read(e_cj)
            sd_evs = []
            for p in range(NCH):
                lam_b = lam[:, l, p:p + 1].broadcast_to([128, ts])
                e_sn = bld.op("dve", lambda q, p=p, lam_b=lam_b:
                              q.tensor_tensor_scan(
                                  snf[:, p, 1:ts + 1], lam_b,
                                  ekvf[:, p, :], snst[:, p:p + 1],
                                  ALU.mult, ALU.add),
                              deps=[ekv_evs[p], e_cj] + ev_lam_all)
                R_sn.did_write(e_sn)
                R_state.did_read(e_sn)
                R_lam.did_read(e_sn)
                R_ekv.did_read(e_sn)
                e_sd = bld.op("dve", lambda q, p=p, lam_b=lam_b:
                              q.tensor_tensor_scan(
                                  sdf[:, p, 1:ts + 1], lam_b,
                                  ekf[:, p, :], sdst[:, p:p + 1],
                                  ALU.mult, ALU.add), deps=[e_sn])
                R_sd.did_write(e_sd)
                R_state.did_read(e_sd)
                R_lam.did_read(e_sd)
                R_ek.did_read(e_sd)
                sd_evs.append(e_sd)
            e_si = bld.op("dve", lambda q: q.tensor_copy(
                snst[:].rearrange("p (c o) -> p c o", o=1),
                snf[:, :, ts:ts + 1]), deps=R_state.wr_deps() + R_sn.rd_deps())
            R_state.did_write(e_si)
            R_sn.did_read(e_si)
            e_sj = bld.op("dve", lambda q: q.tensor_copy(
                sdst[:].rearrange("p (c o) -> p c o", o=1),
                sdf[:, :, ts:ts + 1]), deps=[e_si] + R_sd.rd_deps())
            R_state.did_write(e_sj)
            R_sd.did_read(e_sj)
            for p in range(NCH):
                e_num = bld.op("dve", lambda q, p=p: q.tensor_tensor(
                    snf[:, p, 0:ts], snf[:, p, 0:ts], dsq[:, p, :],
                    ALU.add), deps=[sd_evs[p], e_si] + R_dsq.rd_deps())
                R_dsq.did_read(e_num)
                R_sn.did_write(e_num)
                e_den = bld.op("dve", lambda q, p=p: q.tensor_tensor(
                    sdf[:, p, 0:ts], sdf[:, p, 0:ts], bp(XB0, p),
                    ALU.add), deps=[e_sj] + R_bp["xb"].rd_deps())
                R_bp["xb"].did_read(e_den)
                R_sd.did_write(e_den)
            for p in range(NCH):
                e_l1 = bld.op("act", lambda q, p=p: q.activation(
                    sdf[:, p, 0:ts], sdf[:, p, 0:ts], AF.Ln),
                    deps=R_sd.rd_deps() + R_sd.wr_deps())
                R_sd.did_write(e_l1)
            for p in range(NCH):
                e_rc = bld.op("act", lambda q, p=p: q.activation(
                    sdf[:, p, 0:ts], sdf[:, p, 0:ts], AF.Exp, scale=-1.0),
                    deps=R_sd.rd_deps() + R_sd.wr_deps())
                R_sd.did_write(e_rc)
            for p in range(NCH):
                br, dr = banks.acquire()
                ev_r = proj_group(lambda j: bp(XR0, j), NCH, br, dr,
                                  R_bp["xr"].rd_deps())
                R_bp["xr"].did_read(ev_r)
                e_sr = bld.op("act", lambda q, p=p, br=br: q.activation(
                    srf[:, p, :], banks.t[br][:, 0:ts], AF.Sigmoid),
                    deps=[ev_r] + R_srf.wr_deps())
                banks.read(br, e_sr)
                R_srf.did_write(e_sr)
            e_wkv = bld.op("dve", lambda q: q.tensor_tensor(
                snf[:, :, 0:ts], snf[:, :, 0:ts], sdf[:, :, 0:ts], ALU.mult),
                deps=R_sd.rd_deps() + R_sn.rd_deps() + R_sn.wr_deps())
            R_sn.did_write(e_wkv)
            R_sd.did_read(e_wkv)
            e_rw = bld.op("dve", lambda q: q.tensor_tensor(
                rwkv[:], snf[:, :, 0:ts], srf[:], ALU.mult),
                deps=[e_wkv] + R_srf.rd_deps() + R_rwkv.rd_deps()
                + R_rwkv.wr_deps())
            R_sn.did_read(e_rw)
            R_srf.did_read(e_rw)
            R_rwkv.did_write(e_rw)
            pump(3)
            for cc in range(NCH):
                bo, do = banks.acquire()
                ev_o = proj_group(lambda j: rwkv[:, j, :], NCH, bo, do,
                                  R_rwkv.rd_deps())
                R_rwkv.did_read(ev_o)
                Rx = R_xres[tau]
                e_add = bld.op("dve", lambda q, cc=cc, bo=bo, t0=t0:
                               q.tensor_tensor(
                                   xres[:, cc, t0:t0 + ts],
                                   xres[:, cc, t0:t0 + ts],
                                   banks.t[bo][:, 0:ts], ALU.add),
                               deps=[ev_o] + Rx.wr_deps() + Rx.rd_deps())
                banks.read(bo, e_add)
                Rx.did_write(e_add)
            ctx2 = ln_front(tau)
            pump(3)
            return ctx2

        def kf_view(fc):
            g, c = fc // 8, fc % 8
            if g == 0:
                return bp(XR0, c)
            if g == 1:
                return srf[:, c, :]
            if g == 2:
                return ekf[:, c, :]
            return ekvf[:, c, :]

        def kf_reg(fc):
            return [R_bp["xr"], R_srf, R_ek, R_ekv][fc // 8]

        def cm_tau(l, tau, ctx2, nxt):
            t0 = tau * ts
            ln_back(ctx2, 1, "h", cv_idx(l, "ln2_w"), cv_idx(l, "ln2_b"))
            e_d = mix_d()
            mix_one(e_d, XK0, cv_idx(l, "cmk"))
            mix_one(e_d, XV0, cv_idx(l, "cmr"))
            pump(3)
            kf_evs = []
            for fc in range(NFC):
                bk, dk = banks.acquire()
                ev_k = proj_group(lambda j: bp(XK0, j), NCH, bk, dk,
                                  R_bp["xk"].rd_deps())
                R_bp["xk"].did_read(ev_k)
                dstreg = kf_reg(fc)
                e_rl = bld.op("act", lambda q, fc=fc, bk=bk: q.activation(
                    kf_view(fc), banks.t[bk][:, 0:ts], AF.Relu),
                    deps=[ev_k] + dstreg.wr_deps())
                banks.read(bk, e_rl)
                e_kf = bld.op("act", lambda q, fc=fc: q.activation(
                    kf_view(fc), kf_view(fc), AF.Square), deps=[e_rl])
                dstreg.did_write(e_kf)
                kf_evs.append(e_kf)
                if fc == 3 and nxt is not None:
                    if nxt.get("newlayer"):
                        e_z = bld.op("dve", lambda q: q.memset(
                            bcol2[:, 0:1, :], 0.0),
                            deps=R_bcol.wr_deps() + R_bcol.rd_deps())
                        R_bcol.did_write(e_z)
                    nxt["ctx"] = ln_front(nxt["tau"])
            pump(3)
            if nxt is not None:
                ln_back(nxt["ctx"], 0, "h", cv_idx(nxt["l"], "ln1_w"),
                        cv_idx(nxt["l"], "ln1_b"))
                nxt["e_d"] = mix_d()
            crs_evs = []
            for cc in range(NCH):
                bcr, dcr = banks.acquire()
                ev_cr = proj_group(lambda j: bp(XV0, j), NCH, bcr, dcr,
                                   R_bp["xv"].rd_deps())
                R_bp["xv"].did_read(ev_cr)
                e_sig = bld.op("act", lambda q, cc=cc, bcr=bcr:
                               q.activation(rwkv[:, cc, :],
                                            banks.t[bcr][:, 0:ts],
                                            AF.Sigmoid),
                               deps=[ev_cr] + R_rwkv.wr_deps())
                banks.read(bcr, e_sig)
                R_rwkv.did_write(e_sig)
                crs_evs.append(e_sig)
            if nxt is not None:
                mix_one(nxt["e_d"], XK0, cv_idx(nxt["l"], "tmk"))
                mix_one(nxt["e_d"], XV0, cv_idx(nxt["l"], "tmv"))
            pump(3)
            Rx = R_xres[tau]
            for cc in range(NCH):
                bkv, dkv = banks.acquire()
                ev_kv = proj_group(lambda j: kf_view(j), NFC, bkv, dkv,
                                   kf_evs)
                for fc in range(0, NFC, 8):
                    kf_reg(fc).did_read(ev_kv)
                e_t = bld.op("dve", lambda q, cc=cc, bkv=bkv:
                             q.tensor_tensor(sdf[:, cc % 2, 0:ts],
                                             rwkv[:, cc, :],
                                             banks.t[bkv][:, 0:ts], ALU.mult),
                             deps=[crs_evs[cc], ev_kv] + R_sd.wr_deps())
                banks.read(bkv, e_t)
                R_rwkv.did_read(e_t)
                R_sd.did_write(e_t)
                e_add = bld.op("dve", lambda q, cc=cc, t0=t0:
                               q.tensor_tensor(
                                   xres[:, cc, t0:t0 + ts],
                                   xres[:, cc, t0:t0 + ts],
                                   sdf[:, cc % 2, 0:ts], ALU.add),
                               deps=[e_t] + Rx.wr_deps() + Rx.rd_deps())
                R_sd.did_read(e_add)
                Rx.did_write(e_add)
            if nxt is not None:
                mix_one(nxt["e_d"], XR0, cv_idx(nxt["l"], "tmr"))
            pump(3)

        def ffn_tau(tau):
            t0 = tau * ts
            Rx = R_xres[tau]
            e_xi = bld.dma("xin", "act", lambda q, t0=t0: q.dma_start(
                fp_u3(), x_h[:, :, t0:t0 + ts]), deps=R_sn.wr_deps())
            R_sn.did_write(e_xi)
            e_add = bld.op("dve", lambda q, t0=t0: q.tensor_tensor(
                xres[:, :, t0:t0 + ts], xres[:, :, t0:t0 + ts], fp_u3(),
                ALU.add), deps=[e_xi] + Rx.wr_deps() + Rx.rd_deps())
            R_sn.did_read(e_add)
            Rx.did_write(e_add)

            ln_site(tau, 2, "plain_h", ubuf="sn")
            mixes([(XK0, CV_MUK), (XR0, CV_MUR)])
            pump(6)
            for dc in range(NCH):
                br_, dr_ = banks.acquire()
                ev_r = proj_group(lambda j: bp(XR0, j), NCH, br_, dr_,
                                  R_bp["xr"].rd_deps())
                R_bp["xr"].did_read(ev_r)
                e_sig = bld.op("act", lambda q, dc=dc, br_=br_: q.activation(
                    bp(XB0, dc), banks.t[br_][:, 0:ts], AF.Sigmoid),
                    deps=[ev_r] + R_bp["xb"].wr_deps())
                banks.read(br_, e_sig)
                R_bp["xb"].did_write(e_sig)
            for dc in range(NCH):
                bk_, dk_ = banks.acquire()
                ev_k = proj_group(lambda j: bp(XK0, j), NCH, bk_, dk_,
                                  R_bp["xk"].rd_deps())
                R_bp["xk"].did_read(ev_k)
                e_rl = bld.op("act", lambda q, dc=dc, bk_=bk_: q.activation(
                    bp(XV0, dc), banks.t[bk_][:, 0:ts], AF.Relu),
                    deps=[ev_k] + R_bp["xv"].wr_deps())
                banks.read(bk_, e_rl)
                e_kb = bld.op("dve", lambda q, dc=dc, bk_=bk_: q.tensor_tensor(
                    bp(XV0, dc), bp(XV0, dc), banks.t[bk_][:, 0:ts], ALU.mult),
                    deps=[e_rl])
                banks.read(bk_, e_kb)
                R_bp["xv"].did_write(e_kb)
            pump(6)
            for cc in range(NCH):
                slot = cc % 2
                bv_, dv_ = banks.acquire()
                ev_kv = proj_group(lambda j: bp(XV0, j), NCH, bv_, dv_,
                                   R_bp["xv"].rd_deps())
                R_bp["xv"].did_read(ev_kv)
                e_t = bld.op("dve", lambda q, cc=cc, slot=slot, bv_=bv_:
                             q.tensor_tensor(sdf[:, slot, 0:ts],
                                             bp(XB0, cc),
                                             banks.t[bv_][:, 0:ts], ALU.mult),
                             deps=[ev_kv] + R_bp["xb"].rd_deps()
                             + R_sd.wr_deps())
                banks.read(bv_, e_t)
                R_bp["xb"].did_read(e_t)
                R_sd.did_write(e_t)
                e_add2 = bld.op("dve", lambda q, cc=cc, slot=slot, t0=t0:
                                q.tensor_tensor(
                                    xres[:, cc, t0:t0 + ts],
                                    xres[:, cc, t0:t0 + ts],
                                    sdf[:, slot, 0:ts], ALU.add),
                                deps=[e_t] + Rx.wr_deps() + Rx.rd_deps())
                R_sd.did_read(e_add2)
                Rx.did_write(e_add2)

        pump(12)
        if with_prep:
            for tau in range(ntau):
                ln_site(tau, 0, "plain_x", dep_extra=[ev_xload[tau]],
                        ubuf="sn")
                pump(5)
                ln_site(tau, 0, "wb_x", CV_LN0W, CV_LN0B, ubuf="sn")
                pump(5)

        interleave = dbg_parts == ("attn", "cm") and n_layers > 0 and True
        if interleave:
            ln_site(0, 0, "h", cv_idx(0, "ln1_w"), cv_idx(0, "ln1_b"))
            mixes([(XK0, cv_idx(0, "tmk")), (XV0, cv_idx(0, "tmv")),
                   (XR0, cv_idx(0, "tmr"))])
        for l in range(n_layers):
            e0 = bld.op("dve", lambda q: q.memset(snst[:], 0.0),
                        deps=R_state.wr_deps())
            e1 = bld.op("dve", lambda q: q.memset(sdst[:], 0.0), deps=[e0])
            R_state.did_write(e1)
            e2 = bld.op("dve", lambda q: q.memset(bcol2[:, 1:2, :], 0.0),
                        deps=R_bcol.wr_deps() + R_bcol.rd_deps())
            R_bcol.did_write(e2)
            for tau in range(ntau):
                ws_start(l * TILES_PER_LAYER, TILES_PER_LAYER)
                if not interleave:
                    if "attn" in dbg_parts:
                        ln_site(tau, 0, "h", cv_idx(l, "ln1_w"),
                                cv_idx(l, "ln1_b"))
                        mixes([(XK0, cv_idx(l, "tmk")),
                               (XV0, cv_idx(l, "tmv")),
                               (XR0, cv_idx(l, "tmr"))])
                        attn_tau(l, tau)
                    if "cm" in dbg_parts:
                        if "attn" not in dbg_parts:
                            ws_start(l * TILES_PER_LAYER + 256,
                                     TILES_PER_LAYER - 256)
                        ctx2 = ln_front(tau)
                        cm_tau(l, tau, ctx2, None)
                    continue
                ctx2 = attn_tau(l, tau)
                if tau < ntau - 1:
                    nxt = {"l": l, "tau": tau + 1, "newlayer": False}
                elif l < n_layers - 1:
                    nxt = {"l": l + 1, "tau": 0, "newlayer": True}
                else:
                    nxt = None
                cm_tau(l, tau, ctx2, nxt)

        if with_ffn:
            e2 = bld.op("dve", lambda q: q.memset(bcol2[:, 2:3, :], 0.0),
                        deps=R_bcol.wr_deps() + R_bcol.rd_deps())
            R_bcol.did_write(e2)
            for tau in range(ntau):
                ws_start(NLAYERS * TILES_PER_LAYER, FFN_TILES)
                ffn_tau(tau)

        out_deps = []
        for tau in range(ntau):
            Rx = R_xres[tau]
            e3 = bld.dma("out", "gps", lambda q, tau=tau: q.dma_start(
                out_h[:, :, tau * ts:(tau + 1) * ts],
                xres[:, :, tau * ts:(tau + 1) * ts]), deps=Rx.wr_deps())
            Rx.did_read(e3)
            out_deps.append(e3)
        bld.raw_wait("gps", out_deps)

        with nc.Block() as block:
            bld.emit(block)

    return nc



def _pack_weights(inputs):
    order = []
    for l in range(NLAYERS):
        for nm in ("Wk", "Wv", "Wr"):
            for oc in range(8):
                for cc in range(8):
                    order.append((nm, l, oc, cc))
        for nm, no, ncc in (("Wo", 8, 8), ("Wck", 32, 8)):
            for oc in range(no):
                for cc in range(ncc):
                    order.append((nm, l, oc, cc))
        for oc in range(8):
            for cc in range(8):
                order.append(("Wcr", l, oc, cc))
        for oc in range(8):
            for cc in range(32):
                order.append(("Wcv", l, oc, cc))
    for nm, no, ncc in FFN_MATS:
        for oc in range(no):
            for cc in range(ncc):
                order.append((nm, None, oc, cc))
    assert len(order) == NT

    WT = {}
    for l in range(NLAYERS):
        for nm, key in (("Wk", "L0_attn_Wk"), ("Wv", "L0_attn_Wv"),
                        ("Wr", "L0_attn_Wr"), ("Wo", "L0_attn_Wo"),
                        ("Wck", "L0_attn_Wck"), ("Wcv", "L0_attn_Wcv"),
                        ("Wcr", "L0_attn_Wcr")):
            WT[(nm, l)] = np.ascontiguousarray(inputs[key][l].T)
    WT[("fWr", None)] = np.ascontiguousarray(inputs["L0_ffn_W_r"].T)
    WT[("fWk", None)] = np.ascontiguousarray(inputs["L0_ffn_W_k"].T)
    WT[("fWv", None)] = np.ascontiguousarray(inputs["L0_ffn_W_v"].T)

    wsrc = np.empty((NGRP, 128, 1024), dtype=np.float32)
    for i, (nm, l, oc, cc) in enumerate(order):
        tile = WT[(nm, l)][cc * 128:(cc + 1) * 128, oc * 128:(oc + 1) * 128]
        g, ti = i // 8, i % 8
        wsrc[g][:, ti * 128:(ti + 1) * 128] = tile
    return wsrc


def _pack_cvec(inputs):
    V = np.zeros((NCV, C), dtype=np.float32)
    V[CV_LN0W] = inputs["L0_attn_ln0_w"]
    V[CV_LN0B] = inputs["L0_attn_ln0_b"]
    for l in range(NLAYERS):
        V[cv_idx(l, "ln1_w")] = inputs["L0_attn_ln1_w"][l]
        V[cv_idx(l, "ln1_b")] = inputs["L0_attn_ln1_b"][l]
        V[cv_idx(l, "ln2_w")] = inputs["L0_attn_ln2_w"][l]
        V[cv_idx(l, "ln2_b")] = inputs["L0_attn_ln2_b"][l]
        V[cv_idx(l, "td")] = inputs["L0_attn_time_decay"][l]
        V[cv_idx(l, "tf")] = inputs["L0_attn_time_first"][l]
        V[cv_idx(l, "tmk")] = inputs["L0_attn_tmk"][l]
        V[cv_idx(l, "tmv")] = inputs["L0_attn_tmv"][l]
        V[cv_idx(l, "tmr")] = inputs["L0_attn_tmr"][l]
        V[cv_idx(l, "cmk")] = inputs["L0_attn_cmk"][l]
        V[cv_idx(l, "cmr")] = inputs["L0_attn_cmr"][l]
    V[CV_MUR] = inputs["L0_ffn_mu_r"]
    V[CV_MUK] = inputs["L0_ffn_mu_k"]
    return np.ascontiguousarray(V.reshape(NCV, NCH, 128).transpose(2, 0, 1))


def kernel(**inputs):
    inputs = {k: np.asarray(v) for k, v in inputs.items()}
    x = inputs["L0_attn_x"]
    wsrc = _pack_weights(inputs)
    cvec = _pack_cvec(inputs)

    in_maps = []
    for g in range(NCORES):
        b, j = g // 4, g % 4
        e = ENDS[j]
        s = e - TP
        xc = np.ascontiguousarray(
            x[b, s:e, :].T.reshape(NCH, 128, TP).transpose(1, 0, 2))
        in_maps.append({"x": xc, "cvec": cvec, "wsrc": wsrc})

    nc = build_program()
    res = run_bass_kernel_spmd(nc, in_maps, core_ids=list(range(NCORES)))

    out = np.empty((B, T, C), dtype=np.float32)
    for g in range(NCORES):
        b, j = g // 4, g % 4
        e = ENDS[j]
        o = np.asarray(res.results[g]["out"])
        oc = o.transpose(1, 0, 2).reshape(C, TP).T
        real = REALS[j]
        out[b, e - real:e, :] = oc[TP - real:, :]
    return out


# version 20 (final)
# speedup vs baseline: 1.0655x; 1.0097x over previous
import numpy as np
import concourse.bass as bass
import concourse.mybir as mybir
from concourse.bass_utils import run_bass_kernel_spmd
from contextlib import ExitStack

F32 = mybir.dt.float32
BF16 = mybir.dt.bfloat16
AF = mybir.ActivationFunctionType
ALU = mybir.AluOpType

C = 1024
NCH = 8
NLAYERS = 4
NFC = 32
B = 2
T = 8192
NCORES = 8
DELTA = 256
TP = 2048 + 3 * DELTA // 4
TS = 448
NTAU = TP // TS
ENDS = [TP, TP + 1984, TP + 3968, T]
REALS = [TP, 1984, 1984, 1984]
EPS = 1e-5

MATS = [("Wk", 8, 8), ("Wv", 8, 8), ("Wr", 8, 8), ("Wo", 8, 8),
        ("Wck", 32, 8), ("Wcv", 8, 32), ("Wcr", 8, 8)]
FFN_MATS = [("fWr", 8, 8), ("fWk", 8, 8), ("fWv", 8, 8)]
TILES_PER_LAYER = sum(no * nc for _, no, nc in MATS)
FFN_TILES = sum(no * nc for _, no, nc in FFN_MATS)
NT = NLAYERS * TILES_PER_LAYER + FFN_TILES
NBLK = NT // 8
NGRP = NBLK

CV_LN0W, CV_LN0B = 0, 1


def cv_idx(l, name):
    names = ["ln1_w", "ln1_b", "ln2_w", "ln2_b", "td", "tf",
             "tmk", "tmv", "tmr", "cmk", "cmr"]
    return 2 + 11 * l + names.index(name)


CV_MUR = 2 + 11 * NLAYERS
CV_MUK = CV_MUR + 1
NCV = CV_MUK + 1



def _merge(dst, ev):
    if ev is None:
        return
    kind, src, val = ev
    key = (kind, src)
    if val > dst.get(key, -1):
        dst[key] = val


class Region:

    def __init__(self):
        self.w = {}
        self.r = {}

    def rd_deps(self):
        return [("e" if k[0] == "e" else "q", k[1], v) if False else
                (k[0], k[1], v) for k, v in self.w.items()]

    def wr_deps(self):
        out = {(k[0], k[1]): v for k, v in self.w.items()}
        for k, v in self.r.items():
            if v > out.get(k, -1):
                out[k] = v
        return [(k[0], k[1], v) for k, v in out.items()]

    def did_read(self, ev):
        _merge(self.r, ev)

    def did_write(self, ev):
        _merge(self.w, ev)


class Builder:
    ENGINES = ("pe", "act", "dve", "gps", "sp")

    def __init__(self, nc, es):
        self.nc = nc
        self.es = es
        self.cnt = {e: 0 for e in self.ENGINES}
        self.qcnt = {}
        self.plan = {e: [] for e in self.ENGINES}
        self.sem = {}
        for e in self.ENGINES:
            self.sem[("e", e)] = es.enter_context(nc.semaphore("sem_" + e))
        self.waited = {e: {} for e in self.ENGINES}
        self.pe_labels = []

    def _mk_waits(self, eng, deps):
        need = {}
        for d in deps:
            _merge(need, d)
        out = []
        w = self.waited[eng]
        for key, val in sorted(need.items()):
            if val > w.get(key, 0):
                w[key] = val
                out.append((self.sem[key], val))
        return out

    def op(self, eng, fn, deps=(), label=None, ni=1):
        waits = self._mk_waits(eng, deps)
        self.cnt[eng] += 1
        ev = ("e", eng, self.cnt[eng])
        self.plan[eng].append(("op", waits, fn, None))
        if eng == "pe":
            self.pe_labels.append((label or "?", ni))
        return ev

    def dma(self, qname, eng, fn, deps=()):
        if qname not in self.qcnt:
            self.qcnt[qname] = 0
            self.sem[("q", qname)] = self.es.enter_context(
                self.nc.semaphore("q_" + qname))
        prev = ("q", qname, self.qcnt[qname]) if self.qcnt[qname] else None
        waits = self._mk_waits(eng, list(deps) + ([prev] if prev else []))
        self.qcnt[qname] += 16
        ev = ("q", qname, self.qcnt[qname])
        self.plan[eng].append(("dma", waits, fn, self.sem[("q", qname)]))
        return ev

    def raw_wait(self, eng, deps):
        waits = self._mk_waits(eng, deps)
        if waits:
            self.plan[eng].append(("wait", waits, None, None))

    def emit(self, block):
        deco = {"pe": block.tensor, "act": block.scalar, "dve": block.vector,
                "gps": block.gpsimd, "sp": block.sync}

        def mk(eng):
            items = self.plan[eng]
            osem = self.sem[("e", eng)]

            def body(q):
                for kind, waits, fn, qsem in items:
                    for sem, val in waits:
                        q.wait_ge(sem, val)
                    if kind == "op":
                        fn(q).then_inc(osem, 1)
                    elif kind == "dma":
                        fn(q).then_inc(qsem, 16)
            return body
        for eng in self.ENGINES:
            deco[eng](mk(eng))


class Banks:
    def __init__(self, nc, es):
        self.t = [es.enter_context(nc.psum_tensor(f"bank{i}", [128, 512], F32))
                  for i in range(8)]
        self.state = [{"w": {}, "r": {}, "seq": -i} for i in range(8)]
        self.clock = 0

    def acquire(self, pin=False):
        free = [k for k in range(8) if not self.state[k].get("pin")]
        i = min(free, key=lambda k: self.state[k]["seq"])
        st = self.state[i]
        st["pin"] = pin
        deps = [(k[0], k[1], v) for k, v in st["w"].items()]
        deps += [(k[0], k[1], v) for k, v in st["r"].items()]
        self.clock += 1
        st["seq"] = self.clock
        st["w"] = {}
        st["r"] = {}
        return i, deps

    def unpin(self, i):
        self.state[i]["pin"] = False

    def wrote(self, i, ev):
        _merge(self.state[i]["w"], ev)

    def read(self, i, ev):
        _merge(self.state[i]["r"], ev)



def build_program(n_layers=NLAYERS, with_prep=True, with_ffn=True,
                  tp=TP, ts=TS, dbg_parts=("attn", "cm")):
    ntau = tp // ts
    assert ntau * ts == tp

    nc = bass.Bass("TRN2", target_bir_lowering=False)

    x_h = nc.dram_tensor("x", [128, NCH, tp], F32, kind="ExternalInput").ap()
    cv_h = nc.dram_tensor("cvec", [128, NCV, NCH], F32,
                          kind="ExternalInput").ap()
    w_h = nc.dram_tensor("wsrc", [NGRP, 128, 1024], F32,
                         kind="ExternalInput").ap()
    out_h = nc.dram_tensor("out", [128, NCH, tp], F32,
                           kind="ExternalOutput").ap()
    wc_h = nc.dram_tensor("wcache", [NBLK, 128, 1024], BF16,
                          kind="Internal").ap()

    es = ExitStack()
    with es:
        def sb(name, shape, dt):
            return es.enter_context(nc.sbuf_tensor(name, shape, dt))

        xres = sb("xres", [128, NCH, tp], F32)
        bpool = sb("bpool", [128, 32 * ts], BF16)
        h = sb("h", [128, NCH, ts + 1], BF16)
        dsq = sb("dsq", [128, NCH, ts], BF16)
        ekf = sb("ekf", [128, NCH, ts], BF16)
        ekvf = sb("ekvf", [128, NCH, ts], BF16)
        snf = sb("snf", [128, NCH, ts + 1], F32)
        sdf = sb("sdf", [128, NCH, ts + 1], F32)
        srf = sb("srf", [128, NCH, ts], BF16)
        rwkv = sb("rwkv", [128, NCH, ts], BF16)
        wstage = sb("wstage", [128, 2, 1024], F32)
        wconv = sb("wconv", [128, 2, 1024], BF16)
        wring = sb("wring", [128, 8, 1024], BF16)
        cvec = sb("cvs", [128, NCV, NCH], F32)
        cvb = sb("cvb", [128, NCV, NCH], BF16)
        lam = sb("lam", [128, NLAYERS, NCH], F32)
        eu = sb("eu", [128, NLAYERS, NCH], F32)
        snst = sb("snst", [128, NCH], F32)
        sdst = sb("sdst", [128, NCH], F32)
        bcol2 = sb("bcol2", [128, 3, NCH], BF16)
        ones1 = sb("ones1", [128, 1], BF16)
        onesr = sb("onesr", [1, 128], F32)
        rows = sb("rows", [1, 4, ts], F32)
        epsr = sb("epsr", [1, 1], F32)

        bld = Builder(nc, es)
        banks = Banks(nc, es)

        XK0, XV0, XR0, XB0 = (0, NCH * ts, 2 * NCH * ts, 3 * NCH * ts)
        REGNAME = {XK0: "xk", XV0: "xv", XR0: "xr", XB0: "xb"}

        def bp(base, c):
            return bpool[:, base + c * ts: base + (c + 1) * ts]

        def bp3(base, n=NCH):
            return bpool[:, base: base + n * ts].rearrange(
                "p (c t) -> p c t", c=n)

        KF_REG = ["xv", "xb", "dsq", "h"]

        def kf_view(fc):
            g, c = fc // 8, fc % 8
            if g == 0:
                return bp(XV0, c)
            if g == 1:
                return bp(XB0, c)
            if g == 2:
                return dsq[:, c, :]
            return h[:, c, 0:ts]

        def fp_u3():
            return snf[:, :, 0:ts]

        R_xres = [Region() for _ in range(ntau)]
        R_bp = {k: Region() for k in ("xk", "xv", "xr", "xb")}
        R_h = Region()
        R_dsq = Region()
        R_ek = Region()
        R_ekv = Region()
        R_sn = Region()
        R_sd = Region()
        R_srf = Region()
        R_rows = Region()
        R_rwkv = Region()
        R_bcol = Region()
        R_state = Region()
        R_lam = Region()

        ev_cv = bld.dma("cv", "sp", lambda q: q.dma_start(cvec[:], cv_h))
        ev_cvb = bld.op("act", lambda q: q.activation(cvb[:], cvec[:],
                                                      AF.Copy), deps=[ev_cv])
        ev_eps = bld.op("gps", lambda q: q.memset(epsr[:], float(EPS)))
        ev_ones = bld.op("gps", lambda q: q.memset(ones1[:], 1.0))
        ev_onesr = bld.op("gps", lambda q: q.memset(onesr[:], 1.0))
        e = bld.op("dve", lambda q: q.memset(snst[:], 0.0))
        e = bld.op("dve", lambda q: q.memset(sdst[:], 0.0), deps=[e])
        R_state.did_write(e)
        e = bld.op("dve", lambda q: q.memset(bcol2[:], 0.0))
        R_bcol.did_write(e)

        ev_lam = None
        for l in range(n_layers):
            e1 = bld.op("act", lambda q, l=l: q.activation(
                lam[:, l, :], cvec[:, cv_idx(l, "td"), :], AF.Exp),
                deps=[ev_cv])
            e2 = bld.op("act", lambda q, l=l: q.activation(
                lam[:, l, :], lam[:, l, :], AF.Exp, scale=-1.0), deps=[e1])
            e3 = bld.op("act", lambda q, l=l: q.activation(
                eu[:, l, :], cvec[:, cv_idx(l, "tf"), :], AF.Exp),
                deps=[ev_cv])
            R_lam.did_write(e2)
            R_lam.did_write(e3)
            ev_lam = e3 if ev_lam is None else ev_lam
        ev_lam_all = R_lam.rd_deps()

        ev_xload = []
        for t in range(ntau):
            e = bld.dma("xl%d" % t, "act", lambda q, t=t: q.dma_start(
                xres[:, :, t * ts:(t + 1) * ts],
                x_h[:, :, t * ts:(t + 1) * ts]))
            ev_xload.append(e)
            R_xres[t].did_write(e)

        conv = {"g": 0, "conv_ev": [None, None], "wb_ev": [None] * NBLK}

        def pump(n):
            for _ in range(n):
                g = conv["g"]
                if g >= NGRP:
                    return
                conv["g"] += 1
                s = g % 2
                deps = [conv["conv_ev"][s]] if conv["conv_ev"][s] else []
                ev_ld = bld.dma("wst%d" % s, "gps", lambda q, g=g, s=s:
                                q.dma_start(wstage[:, s, :], w_h[g]),
                                deps=deps)
                wb_prev = conv["wb_ev"][g - 2] if g >= 2 else None
                cdeps = [ev_ld] + ([wb_prev] if wb_prev else [])
                ev_cvt = bld.op("gps", lambda q, s=s: q.tensor_copy(
                    wconv[:, s, :], wstage[:, s, :]), deps=cdeps)
                conv["conv_ev"][s] = ev_cvt
                ev_wb = bld.dma("wcv%d" % s, "gps", lambda q, g=g, s=s:
                                q.dma_start(wc_h[g], wconv[:, s, :]),
                                deps=[ev_cvt])
                conv["wb_ev"][g] = ev_wb

        wst = {"pos": 0, "end": 0, "issued": 0,
               "slot_consumers": [{} for _ in range(4)], "dma_ev": {},
               "hold_end": -1, "replay_end": -1}

        def ws_issue_pair(bp_):
            while conv["wb_ev"][2 * bp_] is None or \
                    conv["wb_ev"][2 * bp_ + 1] is None:
                pump(6)
            slot = wst["issued"] % 4
            wst["issued"] += 1
            deps = [(k[0], k[1], v)
                    for k, v in wst["slot_consumers"][slot].items()]
            wst["slot_consumers"][slot] = {}
            deps.append(conv["wb_ev"][2 * bp_])
            deps.append(conv["wb_ev"][2 * bp_ + 1])
            ev = bld.dma("wr%d" % slot, "sp", lambda q, bp_=bp_, slot=slot:
                         q.dma_start(
                             wring[:, 2 * slot:2 * slot + 2, :],
                             wc_h[2 * bp_:2 * bp_ + 2].transpose([1, 0, 2])),
                         deps=deps)
            wst["dma_ev"][bp_] = (ev, slot)
            wst["hi"] = bp_

        def ws_start(tile_base, ntiles):
            wst["pos"] = tile_base
            wst["end"] = tile_base + ntiles
            wst["hold_end"] = -1
            wst["replay_end"] = -1
            wst["dma_ev"] = {}
            wst["hi"] = -1
            p0 = tile_base // 16
            for k in range(2):
                if (p0 + k) * 16 < wst["end"]:
                    ws_issue_pair(p0 + k)

        def ws_hold(ntiles):
            assert wst["pos"] % 16 == 0 and ntiles % 16 == 0
            assert ntiles <= 64
            wst["hold_end"] = wst["pos"] + ntiles
            for bp_ in range(wst["pos"] // 16, wst["hold_end"] // 16):
                if bp_ > wst["hi"]:
                    ws_issue_pair(bp_)

        def ws_replay(mark, ntiles):
            wst["pos"] = mark
            wst["replay_end"] = mark + ntiles
            wst["hold_end"] = -1

        def ws_resume():
            wst["replay_end"] = -1
            p0 = wst["pos"] // 16
            for bp_ in (p0, p0 + 1):
                if bp_ > wst["hi"] and bp_ * 16 < wst["end"]:
                    ws_issue_pair(bp_)

        def ws_tile(consumer_ev):
            t = wst["pos"]
            wst["pos"] += 1
            bp_, ti = t // 16, t % 16
            ev, slot = wst["dma_ev"][bp_]
            _merge(wst["slot_consumers"][slot], consumer_ev)
            if ti == 0 and t >= wst["replay_end"]:
                nb = bp_ + 2
                if nb * 16 < wst["end"] and nb > wst["hi"] and \
                        (wst["hold_end"] < 0 or nb * 16 < wst["hold_end"]):
                    ws_issue_pair(nb)
            return (wring[:, 2 * slot + ti // 8, (ti % 8) * 128:
                          (ti % 8) * 128 + 128], ev)

        def proj_group(moving_fn, nacc, bank, bank_deps, extra_deps=(),
                       label="proj", w=None):
            w = w or ts
            ev_future = ("e", "pe", bld.cnt["pe"] + 1)
            tiles = [ws_tile(ev_future) for _ in range(nacc)]
            deps = list(bank_deps) + [d for _, d in tiles] + \
                list(extra_deps)

            def go(q):
                last = None
                for j, (ap, _) in enumerate(tiles):
                    last = q.matmul(banks.t[bank][:, 0:w], ap, moving_fn(j),
                                    start=(j == 0), stop=(j == nacc - 1))
                return last
            ev = bld.op("pe", go, deps=deps, label=label, ni=nacc)
            assert ev == ev_future
            banks.wrote(bank, ev)
            return ev

        def ln_stats_src(tau, c, deps):
            t0 = tau * ts
            e_xb = bld.op("act", lambda q, c=c, t0=t0: q.activation(
                bp(XB0, c), xres[:, c, t0:t0 + ts], AF.Copy),
                deps=deps + R_bp["xb"].wr_deps())
            R_xres[tau].did_read(e_xb)
            R_bp["xb"].did_write(e_xb)
            e_xsq = bld.op("act", lambda q, c=c, t0=t0: q.activation(
                dsq[:, c, :], xres[:, c, t0:t0 + ts], AF.Square),
                deps=deps + R_dsq.wr_deps())
            R_xres[tau].did_read(e_xsq)
            R_dsq.did_write(e_xsq)

        def ln_front(tau, dep_extra=(), pre=False):
            t0 = tau * ts
            xt = xres[:, :, t0:t0 + ts]
            Rx = R_xres[tau]
            deps0 = Rx.rd_deps() + list(dep_extra)
            if pre:
                e_xb = ("e", "act", bld.cnt["act"])
                e_xsq = e_xb
            else:
                e_xb = bld.op("act", lambda q: q.activation(
                    bp3(XB0), xt, AF.Copy), deps=deps0 + R_bp["xb"].wr_deps())
                Rx.did_read(e_xb)
                R_bp["xb"].did_write(e_xb)
                e_xsq = bld.op("act", lambda q: q.activation(
                    dsq[:], xt, AF.Square), deps=deps0 + R_dsq.wr_deps())
                Rx.did_read(e_xsq)
                R_dsq.did_write(e_xsq)

            b0, d0 = banks.acquire(pin=True)
            b1, d1 = banks.acquire(pin=True)

            def mm_stats(q, bank, use_dsq):
                last = None
                for c in range(NCH):
                    mv = dsq[:, c, :] if use_dsq else bp(XB0, c)
                    last = q.matmul(banks.t[bank][0:1, 0:ts], ones1[:], mv,
                                    start=(c == 0), stop=(c == NCH - 1))
                return last
            e_s0 = bld.op("pe", lambda q, b0=b0: mm_stats(q, b0, False),
                          deps=d0 + [e_xb, ev_ones] + R_bp["xb"].rd_deps(),
                          label="stats", ni=NCH)
            banks.wrote(b0, e_s0)
            R_bp["xb"].did_read(e_s0)
            e_s1 = bld.op("pe", lambda q, b1=b1: mm_stats(q, b1, True),
                          deps=d1 + [e_xsq] + R_dsq.rd_deps(),
                          label="stats", ni=NCH)
            banks.wrote(b1, e_s1)
            R_dsq.did_read(e_s1)

            rdeps = R_rows.wr_deps() + R_rows.rd_deps()
            e_rm = bld.op("dve", lambda q: q.tensor_scalar_mul(
                rows[:, 0, :], banks.t[b0][0:1, 0:ts], 1.0 / C),
                deps=rdeps + [e_s0])
            banks.read(b0, e_rm)
            e_rv = bld.op("dve", lambda q: q.tensor_scalar_mul(
                rows[:, 2, :], banks.t[b1][0:1, 0:ts], 1.0 / C),
                deps=rdeps + [e_s1])
            banks.read(b1, e_rv)
            banks.unpin(b0)
            banks.unpin(b1)
            e_msq = bld.op("dve", lambda q: q.tensor_tensor(
                rows[:, 1, :], rows[:, 0, :], rows[:, 0, :], ALU.mult),
                deps=[e_rm])
            e_var = bld.op("dve", lambda q: q.tensor_tensor(
                rows[:, 2, :], rows[:, 2, :], rows[:, 1, :], ALU.subtract),
                deps=[e_rv, e_msq])
            e_sq = bld.op("act", lambda q: q.activation(
                rows[:, 3, :], rows[:, 2, :], AF.Ln, bias=epsr[:]),
                deps=[e_var, ev_eps])
            e_rstd = bld.op("act", lambda q: q.activation(
                rows[:, 1, :], rows[:, 3, :], AF.Exp, scale=-0.5),
                deps=[e_sq])
            e_brow = bld.op("dve", lambda q: q.scalar_tensor_tensor(
                rows[:, 0, :], rows[:, 0, :], -1.0, rows[:, 1, :],
                ALU.mult, ALU.mult), deps=[e_rstd])
            R_rows.did_write(e_brow)

            ba, da = banks.acquire(pin=True)
            bb, db = banks.acquire(pin=True)
            e_bca = bld.op("pe", lambda q, ba=ba: q.matmul(
                banks.t[ba][:, 0:ts], onesr[:], rows[:, 1, :],
                start=True, stop=True), deps=da + [e_rstd, ev_onesr],
                label="bcast")
            banks.wrote(ba, e_bca)
            R_rows.did_read(e_bca)
            e_bcb = bld.op("pe", lambda q, bb=bb: q.matmul(
                banks.t[bb][:, 0:ts], onesr[:], rows[:, 0, :],
                start=True, stop=True), deps=db + [e_brow, ev_onesr],
                label="bcast")
            banks.wrote(bb, e_bcb)
            R_rows.did_read(e_bcb)
            return dict(tau=tau, ba=ba, bb=bb, e_bca=e_bca, e_bcb=e_bcb)

        def ln_back(ctx, site, mode, wvec=None, bvec=None, ubuf="sn"):
            tau = ctx["tau"]
            t0 = tau * ts
            xt = xres[:, :, t0:t0 + ts]
            Rx = R_xres[tau]
            ba, bb = ctx["ba"], ctx["bb"]
            fs0 = R_ek if ubuf == "ek" else R_sn
            u = ekf[:] if ubuf == "ek" else fp_u3()
            udeps = fs0.wr_deps()
            ue = []
            for c in range(NCH):
                e1 = bld.op("dve", lambda q, c=c, ba=ba: q.tensor_tensor(
                    u[:, c, :], xt[:, c, :], banks.t[ba][:, 0:ts], ALU.mult),
                    deps=udeps + [ctx["e_bca"]])
                banks.read(ba, e1)
                Rx.did_read(e1)
                e2 = bld.op("dve", lambda q, c=c, bb=bb: q.tensor_tensor(
                    u[:, c, :], u[:, c, :], banks.t[bb][:, 0:ts], ALU.add),
                    deps=[e1, ctx["e_bcb"]])
                banks.read(bb, e2)
                ue.append(e2)
            for e2 in ue:
                fs0.did_write(e2)
            banks.unpin(ba)
            banks.unpin(bb)

            if mode == "plain_x":
                e_o = bld.op("dve", lambda q: q.tensor_copy(xt, u),
                             deps=ue + Rx.wr_deps())
                fs0.did_read(e_o)
                Rx.did_write(e_o)
                return
            if mode == "wb_x":
                wdeps = ue + [ev_cv] + Rx.wr_deps()
                for c in range(NCH):
                    e3 = bld.op("act", lambda q, c=c: q.activation(
                        xt[:, c, :], u[:, c, :], AF.Identity,
                        bias=cvec[:, bvec, c:c + 1],
                        scale=cvec[:, wvec, c:c + 1]), deps=wdeps)
                    fs0.did_read(e3)
                    Rx.did_write(e3)
                return
            e_bc = bld.op("dve", lambda q: q.tensor_copy(
                h[:, :, 0:1], bcol2[:, site:site + 1, :].transpose([0, 2, 1])),
                deps=R_h.wr_deps() + R_h.rd_deps() + R_bcol.rd_deps())
            R_bcol.did_read(e_bc)
            R_h.did_write(e_bc)
            evs = []
            if mode == "plain_h":
                e3 = bld.op("dve", lambda q: q.tensor_copy(
                    h[:, :, 1:ts + 1], u), deps=ue + [e_bc])
                fs0.did_read(e3)
                evs.append(e3)
            else:
                for c in range(NCH):
                    e3 = bld.op("act", lambda q, c=c: q.activation(
                        h[:, c, 1:ts + 1], u[:, c, :], AF.Identity,
                        bias=cvec[:, bvec, c:c + 1],
                        scale=cvec[:, wvec, c:c + 1]),
                        deps=ue + [ev_cv, e_bc])
                    fs0.did_read(e3)
                    evs.append(e3)
            for e3 in evs:
                R_h.did_write(e3)
            e_sv = bld.op("dve", lambda q: q.tensor_copy(
                bcol2[:, site:site + 1, :],
                h[:, :, ts:ts + 1].transpose([0, 2, 1])),
                deps=evs + R_bcol.wr_deps())
            R_bcol.did_write(e_sv)
            R_h.did_read(e_sv)

        def ln_site(tau, site, mode, wvec=None, bvec=None, dep_extra=(),
                    ubuf="sn"):
            ctx = ln_front(tau, dep_extra)
            ln_back(ctx, site, mode, wvec, bvec, ubuf)

        def mix_d():
            e_d = bld.op("dve", lambda q: q.tensor_tensor(
                dsq[:], h[:, :, 1:ts + 1], h[:, :, 0:ts], ALU.subtract),
                deps=R_h.rd_deps() + R_dsq.wr_deps())
            R_h.did_read(e_d)
            R_dsq.did_write(e_d)
            return e_d

        def mix_one(e_d, base, row):
            reg = R_bp[REGNAME[base]]
            for c in range(NCH):
                e3 = bld.op("dve", lambda q, c=c, base=base, row=row:
                            q.scalar_tensor_tensor(
                                bp(base, c), dsq[:, c, :],
                                cvb[:, row, c:c + 1], h[:, c, 0:ts],
                                ALU.mult, ALU.add),
                            deps=[e_d, ev_cvb] + reg.wr_deps())
                R_dsq.did_read(e3)
                R_h.did_read(e3)
                reg.did_write(e3)

        def mixes(pairs):
            e_d = mix_d()
            for base, row in pairs:
                mix_one(e_d, base, row)

        def attn_tau(l, tau):
            t0 = tau * ts
            pump(3)
            for p in range(NCH):
                bk, dk = banks.acquire(pin=True)
                ek_ev = proj_group(lambda j: bp(XK0, j), NCH, bk, dk,
                                   R_bp["xk"].rd_deps(), label="k")
                R_bp["xk"].did_read(ek_ev)
                e_ek = bld.op("act", lambda q, p=p, bk=bk:
                              q.activation(ekf[:, p, :],
                                           banks.t[bk][:, 0:ts], AF.Exp),
                              deps=[ek_ev] + R_ek.wr_deps())
                banks.read(bk, e_ek)
                R_ek.did_write(e_ek)
                e_eku = bld.op("act", lambda q, p=p, bk=bk, l=l: q.activation(
                    bp(XB0, p), banks.t[bk][:, 0:ts], AF.Exp,
                    bias=cvec[:, cv_idx(l, "tf"), p:p + 1]),
                    deps=[ek_ev, ev_cv] + R_bp["xb"].wr_deps())
                banks.read(bk, e_eku)
                banks.unpin(bk)
                R_bp["xb"].did_write(e_eku)
            e_ci = bld.op("dve", lambda q: q.tensor_copy(
                snf[:, :, 0:1],
                snst[:].rearrange("p (c o) -> p c o", o=1)),
                deps=R_state.rd_deps() + R_sn.wr_deps())
            R_sn.did_write(e_ci)
            R_state.did_read(e_ci)
            e_cj = bld.op("dve", lambda q: q.tensor_copy(
                sdf[:, :, 0:1],
                sdst[:].rearrange("p (c o) -> p c o", o=1)),
                deps=[e_ci] + R_sd.wr_deps())
            R_sd.did_write(e_cj)
            R_state.did_read(e_cj)
            for p in range(NCH):
                bv, dv = banks.acquire()
                ev_v = proj_group(lambda j: bp(XV0, j), NCH, bv, dv,
                                  R_bp["xv"].rd_deps(), label="v")
                R_bp["xv"].did_read(ev_v)
                e_ekv = bld.op("dve", lambda q, p=p, bv=bv: q.tensor_tensor(
                    ekvf[:, p, :], ekf[:, p, :],
                    banks.t[bv][:, 0:ts], ALU.mult),
                    deps=[ev_v] + R_ekv.wr_deps() + R_ek.rd_deps())
                banks.read(bv, e_ekv)
                R_ek.did_read(e_ekv)
                R_ekv.did_write(e_ekv)
                e_ekuv = bld.op("dve", lambda q, p=p, bv=bv: q.tensor_tensor(
                    dsq[:, p, :], bp(XB0, p),
                    banks.t[bv][:, 0:ts], ALU.mult),
                    deps=[ev_v] + R_dsq.wr_deps() + R_bp["xb"].rd_deps())
                banks.read(bv, e_ekuv)
                R_bp["xb"].did_read(e_ekuv)
                R_dsq.did_write(e_ekuv)
                lam_b = lam[:, l, p:p + 1].broadcast_to([128, ts])
                e_sn = bld.op("dve", lambda q, p=p, lam_b=lam_b:
                              q.tensor_tensor_scan(
                                  snf[:, p, 1:ts + 1], lam_b,
                                  ekvf[:, p, :], snf[:, p, 0:1],
                                  ALU.mult, ALU.add),
                              deps=[e_ekv, e_cj] + ev_lam_all
                              + R_sn.rd_deps())
                R_sn.did_write(e_sn)
                R_lam.did_read(e_sn)
                R_ekv.did_read(e_sn)
                e_sd = bld.op("dve", lambda q, p=p, lam_b=lam_b:
                              q.tensor_tensor_scan(
                                  sdf[:, p, 1:ts + 1], lam_b,
                                  ekf[:, p, :], sdf[:, p, 0:1],
                                  ALU.mult, ALU.add), deps=[e_sn])
                R_sd.did_write(e_sd)
                R_lam.did_read(e_sd)
                R_ek.did_read(e_sd)
                e_num = bld.op("dve", lambda q, p=p: q.tensor_tensor(
                    snf[:, p, 0:ts], snf[:, p, 0:ts],
                    dsq[:, p, :], ALU.add),
                    deps=[e_sd] + R_dsq.rd_deps() + R_state.rd_deps())
                R_dsq.did_read(e_num)
                R_sn.did_write(e_num)
                e_den = bld.op("dve", lambda q, p=p: q.tensor_tensor(
                    sdf[:, p, 0:ts], sdf[:, p, 0:ts],
                    bp(XB0, p), ALU.add),
                    deps=[e_sd] + R_bp["xb"].rd_deps() + R_state.rd_deps())
                R_bp["xb"].did_read(e_den)
                R_sd.did_write(e_den)
                e_l1 = bld.op("act", lambda q, p=p: q.activation(
                    sdf[:, p, 0:ts], sdf[:, p, 0:ts], AF.Ln),
                    deps=[e_den] + R_sd.rd_deps())
                R_sd.did_write(e_l1)
                e_rc = bld.op("act", lambda q, p=p: q.activation(
                    sdf[:, p, 0:ts], sdf[:, p, 0:ts], AF.Exp,
                    scale=-1.0), deps=[e_l1])
                R_sd.did_write(e_rc)
            e_si = bld.op("dve", lambda q: q.tensor_copy(
                snst[:].rearrange("p (c o) -> p c o", o=1),
                snf[:, :, ts:ts + 1]),
                deps=R_state.wr_deps() + R_sn.rd_deps() + R_sn.wr_deps())
            R_state.did_write(e_si)
            R_sn.did_read(e_si)
            e_sj = bld.op("dve", lambda q: q.tensor_copy(
                sdst[:].rearrange("p (c o) -> p c o", o=1),
                sdf[:, :, ts:ts + 1]), deps=[e_si] + R_sd.rd_deps())
            R_state.did_write(e_sj)
            R_sd.did_read(e_sj)
            for p in range(NCH):
                br, dr = banks.acquire()
                ev_r = proj_group(lambda j: bp(XR0, j), NCH, br, dr,
                                  R_bp["xr"].rd_deps(), label="r")
                R_bp["xr"].did_read(ev_r)
                e_sr = bld.op("act", lambda q, p=p, br=br: q.activation(
                    srf[:, p, :], banks.t[br][:, 0:ts], AF.Sigmoid),
                    deps=[ev_r] + R_srf.wr_deps())
                banks.read(br, e_sr)
                R_srf.did_write(e_sr)
                e_wkv = bld.op("dve", lambda q, p=p: q.tensor_tensor(
                    snf[:, p, 0:ts], snf[:, p, 0:ts],
                    sdf[:, p, 0:ts], ALU.mult),
                    deps=R_sd.rd_deps() + R_sd.wr_deps() + R_sn.rd_deps()
                    + R_sn.wr_deps())
                R_sn.did_write(e_wkv)
                R_sd.did_read(e_wkv)
                e_rw = bld.op("dve", lambda q, p=p: q.tensor_tensor(
                    rwkv[:, p, :], snf[:, p, 0:ts], srf[:, p, :], ALU.mult),
                    deps=[e_wkv, e_sr] + R_rwkv.rd_deps()
                    + R_rwkv.wr_deps())
                R_sn.did_read(e_rw)
                R_srf.did_read(e_rw)
                R_rwkv.did_write(e_rw)
            pump(3)
            for cc in range(NCH):
                bo, do = banks.acquire()
                ev_o = proj_group(lambda j: rwkv[:, j, :], NCH, bo, do,
                                  R_rwkv.rd_deps(), label="Wo")
                R_rwkv.did_read(ev_o)
                Rx = R_xres[tau]
                e_add = bld.op("dve", lambda q, cc=cc, bo=bo, t0=t0:
                               q.tensor_tensor(
                                   xres[:, cc, t0:t0 + ts],
                                   xres[:, cc, t0:t0 + ts],
                                   banks.t[bo][:, 0:ts], ALU.add),
                               deps=[ev_o] + Rx.wr_deps() + Rx.rd_deps())
                banks.read(bo, e_add)
                Rx.did_write(e_add)
            ctx2 = ln_front(tau)
            pump(3)
            return ctx2

        def kf_view(fc):
            g, c = fc // 8, fc % 8
            if g == 0:
                return bp(XR0, c)
            if g == 1:
                return srf[:, c, :]
            if g == 2:
                return ekf[:, c, :]
            return ekvf[:, c, :]

        def kf_reg(fc):
            return [R_bp["xr"], R_srf, R_ek, R_ekv][fc // 8]

        def cm_tau(l, tau, ctx2, nxt):
            t0 = tau * ts
            ln_back(ctx2, 1, "h", cv_idx(l, "ln2_w"), cv_idx(l, "ln2_b"))
            e_d = mix_d()
            mix_one(e_d, XK0, cv_idx(l, "cmk"))
            mix_one(e_d, XV0, cv_idx(l, "cmr"))
            pump(3)
            kf_evs = []
            for fc in range(NFC):
                bk, dk = banks.acquire()
                ev_k = proj_group(lambda j: bp(XK0, j), NCH, bk, dk,
                                  R_bp["xk"].rd_deps(), label="kf")
                R_bp["xk"].did_read(ev_k)
                dstreg = kf_reg(fc)
                e_rl = bld.op("act", lambda q, fc=fc, bk=bk: q.activation(
                    kf_view(fc), banks.t[bk][:, 0:ts], AF.Relu),
                    deps=[ev_k] + dstreg.wr_deps())
                banks.read(bk, e_rl)
                e_kf = bld.op("act", lambda q, fc=fc: q.activation(
                    kf_view(fc), kf_view(fc), AF.Square), deps=[e_rl])
                dstreg.did_write(e_kf)
                kf_evs.append(e_kf)
                if fc == 3 and nxt is not None:
                    if nxt.get("newlayer"):
                        e_z = bld.op("dve", lambda q: q.memset(
                            bcol2[:, 0:1, :], 0.0),
                            deps=R_bcol.wr_deps() + R_bcol.rd_deps())
                        R_bcol.did_write(e_z)
                    nxt["ctx"] = ln_front(nxt["tau"])
            pump(3)
            if nxt is not None:
                ln_back(nxt["ctx"], 0, "h", cv_idx(nxt["l"], "ln1_w"),
                        cv_idx(nxt["l"], "ln1_b"))
                nxt["e_d"] = mix_d()
            crs_evs = []
            for cc in range(NCH):
                bcr, dcr = banks.acquire()
                ev_cr = proj_group(lambda j: bp(XV0, j), NCH, bcr, dcr,
                                   R_bp["xv"].rd_deps(), label="cr")
                R_bp["xv"].did_read(ev_cr)
                e_sig = bld.op("act", lambda q, cc=cc, bcr=bcr:
                               q.activation(rwkv[:, cc, :],
                                            banks.t[bcr][:, 0:ts],
                                            AF.Sigmoid),
                               deps=[ev_cr] + R_rwkv.wr_deps())
                banks.read(bcr, e_sig)
                R_rwkv.did_write(e_sig)
                crs_evs.append(e_sig)
            if nxt is not None:
                mix_one(nxt["e_d"], XK0, cv_idx(nxt["l"], "tmk"))
                mix_one(nxt["e_d"], XV0, cv_idx(nxt["l"], "tmv"))
            pump(3)
            Rx = R_xres[tau]
            for cc in range(NCH):
                bkv, dkv = banks.acquire()
                ev_kv = proj_group(lambda j: kf_view(j), NFC, bkv, dkv,
                                   kf_evs, label="kv")
                for fc in range(0, NFC, 8):
                    kf_reg(fc).did_read(ev_kv)
                e_t = bld.op("dve", lambda q, cc=cc, bkv=bkv:
                             q.tensor_tensor(sdf[:, cc % 2, 0:ts],
                                             rwkv[:, cc, :],
                                             banks.t[bkv][:, 0:ts], ALU.mult),
                             deps=[crs_evs[cc], ev_kv] + R_sd.wr_deps())
                banks.read(bkv, e_t)
                R_rwkv.did_read(e_t)
                R_sd.did_write(e_t)
                e_add = bld.op("dve", lambda q, cc=cc, t0=t0:
                               q.tensor_tensor(
                                   xres[:, cc, t0:t0 + ts],
                                   xres[:, cc, t0:t0 + ts],
                                   sdf[:, cc % 2, 0:ts], ALU.add),
                               deps=[e_t] + Rx.wr_deps() + Rx.rd_deps())
                R_sd.did_read(e_add)
                Rx.did_write(e_add)
            if nxt is not None:
                mix_one(nxt["e_d"], XR0, cv_idx(nxt["l"], "tmr"))
            pump(3)

        def ffn_tau(tau):
            t0 = tau * ts
            Rx = R_xres[tau]
            e_xi = bld.dma("xin", "act", lambda q, t0=t0: q.dma_start(
                fp_u3(), x_h[:, :, t0:t0 + ts]), deps=R_sn.wr_deps())
            R_sn.did_write(e_xi)
            e_add = bld.op("dve", lambda q, t0=t0: q.tensor_tensor(
                xres[:, :, t0:t0 + ts], xres[:, :, t0:t0 + ts], fp_u3(),
                ALU.add), deps=[e_xi] + Rx.wr_deps() + Rx.rd_deps())
            R_sn.did_read(e_add)
            Rx.did_write(e_add)

            ln_site(tau, 2, "plain_h", ubuf="sn")
            mixes([(XK0, CV_MUK), (XR0, CV_MUR)])
            pump(6)
            for dc in range(NCH):
                br_, dr_ = banks.acquire()
                ev_r = proj_group(lambda j: bp(XR0, j), NCH, br_, dr_,
                                  R_bp["xr"].rd_deps())
                R_bp["xr"].did_read(ev_r)
                e_sig = bld.op("act", lambda q, dc=dc, br_=br_: q.activation(
                    bp(XB0, dc), banks.t[br_][:, 0:ts], AF.Sigmoid),
                    deps=[ev_r] + R_bp["xb"].wr_deps())
                banks.read(br_, e_sig)
                R_bp["xb"].did_write(e_sig)
            for dc in range(NCH):
                bk_, dk_ = banks.acquire()
                ev_k = proj_group(lambda j: bp(XK0, j), NCH, bk_, dk_,
                                  R_bp["xk"].rd_deps())
                R_bp["xk"].did_read(ev_k)
                e_rl = bld.op("act", lambda q, dc=dc, bk_=bk_: q.activation(
                    bp(XV0, dc), banks.t[bk_][:, 0:ts], AF.Relu),
                    deps=[ev_k] + R_bp["xv"].wr_deps())
                banks.read(bk_, e_rl)
                e_kb = bld.op("dve", lambda q, dc=dc, bk_=bk_: q.tensor_tensor(
                    bp(XV0, dc), bp(XV0, dc), banks.t[bk_][:, 0:ts], ALU.mult),
                    deps=[e_rl])
                banks.read(bk_, e_kb)
                R_bp["xv"].did_write(e_kb)
            pump(6)
            for cc in range(NCH):
                slot = cc % 2
                bv_, dv_ = banks.acquire()
                ev_kv = proj_group(lambda j: bp(XV0, j), NCH, bv_, dv_,
                                   R_bp["xv"].rd_deps())
                R_bp["xv"].did_read(ev_kv)
                e_t = bld.op("dve", lambda q, cc=cc, slot=slot, bv_=bv_:
                             q.tensor_tensor(sdf[:, slot, 0:ts],
                                             bp(XB0, cc),
                                             banks.t[bv_][:, 0:ts], ALU.mult),
                             deps=[ev_kv] + R_bp["xb"].rd_deps()
                             + R_sd.wr_deps())
                banks.read(bv_, e_t)
                R_bp["xb"].did_read(e_t)
                R_sd.did_write(e_t)
                e_add2 = bld.op("dve", lambda q, cc=cc, slot=slot, t0=t0:
                                q.tensor_tensor(
                                    xres[:, cc, t0:t0 + ts],
                                    xres[:, cc, t0:t0 + ts],
                                    sdf[:, slot, 0:ts], ALU.add),
                                deps=[e_t] + Rx.wr_deps() + Rx.rd_deps())
                R_sd.did_read(e_add2)
                Rx.did_write(e_add2)

        pump(12)
        if with_prep:
            for tau in range(ntau):
                ln_site(tau, 0, "plain_x", dep_extra=[ev_xload[tau]],
                        ubuf="sn")
                pump(5)
                ln_site(tau, 0, "wb_x", CV_LN0W, CV_LN0B, ubuf="sn")
                pump(5)

        interleave = dbg_parts == ("attn", "cm") and n_layers > 0 and True
        if interleave:
            ln_site(0, 0, "h", cv_idx(0, "ln1_w"), cv_idx(0, "ln1_b"))
            mixes([(XK0, cv_idx(0, "tmk")), (XV0, cv_idx(0, "tmv")),
                   (XR0, cv_idx(0, "tmr"))])
        for l in range(n_layers):
            e0 = bld.op("dve", lambda q: q.memset(snst[:], 0.0),
                        deps=R_state.wr_deps())
            e1 = bld.op("dve", lambda q: q.memset(sdst[:], 0.0), deps=[e0])
            R_state.did_write(e1)
            e2 = bld.op("dve", lambda q: q.memset(bcol2[:, 1:2, :], 0.0),
                        deps=R_bcol.wr_deps() + R_bcol.rd_deps())
            R_bcol.did_write(e2)
            for tau in range(ntau):
                ws_start(l * TILES_PER_LAYER, TILES_PER_LAYER)
                if not interleave:
                    if "attn" in dbg_parts:
                        ln_site(tau, 0, "h", cv_idx(l, "ln1_w"),
                                cv_idx(l, "ln1_b"))
                        mixes([(XK0, cv_idx(l, "tmk")),
                               (XV0, cv_idx(l, "tmv")),
                               (XR0, cv_idx(l, "tmr"))])
                        attn_tau(l, tau)
                    if "cm" in dbg_parts:
                        if "attn" not in dbg_parts:
                            ws_start(l * TILES_PER_LAYER + 256,
                                     TILES_PER_LAYER - 256)
                        ctx2 = ln_front(tau)
                        cm_tau(l, tau, ctx2, None)
                    continue
                ctx2 = attn_tau(l, tau)
                if tau < ntau - 1:
                    nxt = {"l": l, "tau": tau + 1, "newlayer": False}
                elif l < n_layers - 1:
                    nxt = {"l": l + 1, "tau": 0, "newlayer": True}
                else:
                    nxt = None
                cm_tau(l, tau, ctx2, nxt)

        if with_ffn:
            e2 = bld.op("dve", lambda q: q.memset(bcol2[:, 2:3, :], 0.0),
                        deps=R_bcol.wr_deps() + R_bcol.rd_deps())
            R_bcol.did_write(e2)
            for tau in range(ntau):
                ws_start(NLAYERS * TILES_PER_LAYER, FFN_TILES)
                ffn_tau(tau)

        out_deps = []
        for tau in range(ntau):
            Rx = R_xres[tau]
            e3 = bld.dma("out", "gps", lambda q, tau=tau: q.dma_start(
                out_h[:, :, tau * ts:(tau + 1) * ts],
                xres[:, :, tau * ts:(tau + 1) * ts]), deps=Rx.wr_deps())
            Rx.did_read(e3)
            out_deps.append(e3)
        bld.raw_wait("gps", out_deps)

        with nc.Block() as block:
            bld.emit(block)

    return nc



def _pack_weights(inputs):
    order = []
    for l in range(NLAYERS):
        for nm in ("Wk", "Wv", "Wr"):
            for oc in range(8):
                for cc in range(8):
                    order.append((nm, l, oc, cc))
        for nm, no, ncc in (("Wo", 8, 8), ("Wck", 32, 8)):
            for oc in range(no):
                for cc in range(ncc):
                    order.append((nm, l, oc, cc))
        for oc in range(8):
            for cc in range(8):
                order.append(("Wcr", l, oc, cc))
        for oc in range(8):
            for cc in range(32):
                order.append(("Wcv", l, oc, cc))
    for nm, no, ncc in FFN_MATS:
        for oc in range(no):
            for cc in range(ncc):
                order.append((nm, None, oc, cc))
    assert len(order) == NT

    WT = {}
    for l in range(NLAYERS):
        for nm, key in (("Wk", "L0_attn_Wk"), ("Wv", "L0_attn_Wv"),
                        ("Wr", "L0_attn_Wr"), ("Wo", "L0_attn_Wo"),
                        ("Wck", "L0_attn_Wck"), ("Wcv", "L0_attn_Wcv"),
                        ("Wcr", "L0_attn_Wcr")):
            WT[(nm, l)] = np.ascontiguousarray(inputs[key][l].T)
    WT[("fWr", None)] = np.ascontiguousarray(inputs["L0_ffn_W_r"].T)
    WT[("fWk", None)] = np.ascontiguousarray(inputs["L0_ffn_W_k"].T)
    WT[("fWv", None)] = np.ascontiguousarray(inputs["L0_ffn_W_v"].T)

    wsrc = np.empty((NGRP, 128, 1024), dtype=np.float32)
    for i, (nm, l, oc, cc) in enumerate(order):
        tile = WT[(nm, l)][cc * 128:(cc + 1) * 128, oc * 128:(oc + 1) * 128]
        g, ti = i // 8, i % 8
        wsrc[g][:, ti * 128:(ti + 1) * 128] = tile
    return wsrc


def _pack_cvec(inputs):
    V = np.zeros((NCV, C), dtype=np.float32)
    V[CV_LN0W] = inputs["L0_attn_ln0_w"]
    V[CV_LN0B] = inputs["L0_attn_ln0_b"]
    for l in range(NLAYERS):
        V[cv_idx(l, "ln1_w")] = inputs["L0_attn_ln1_w"][l]
        V[cv_idx(l, "ln1_b")] = inputs["L0_attn_ln1_b"][l]
        V[cv_idx(l, "ln2_w")] = inputs["L0_attn_ln2_w"][l]
        V[cv_idx(l, "ln2_b")] = inputs["L0_attn_ln2_b"][l]
        V[cv_idx(l, "td")] = inputs["L0_attn_time_decay"][l]
        V[cv_idx(l, "tf")] = inputs["L0_attn_time_first"][l]
        V[cv_idx(l, "tmk")] = inputs["L0_attn_tmk"][l]
        V[cv_idx(l, "tmv")] = inputs["L0_attn_tmv"][l]
        V[cv_idx(l, "tmr")] = inputs["L0_attn_tmr"][l]
        V[cv_idx(l, "cmk")] = inputs["L0_attn_cmk"][l]
        V[cv_idx(l, "cmr")] = inputs["L0_attn_cmr"][l]
    V[CV_MUR] = inputs["L0_ffn_mu_r"]
    V[CV_MUK] = inputs["L0_ffn_mu_k"]
    return np.ascontiguousarray(V.reshape(NCV, NCH, 128).transpose(2, 0, 1))


def kernel(**inputs):
    inputs = {k: np.asarray(v) for k, v in inputs.items()}
    x = inputs["L0_attn_x"]
    wsrc = _pack_weights(inputs)
    cvec = _pack_cvec(inputs)

    in_maps = []
    for g in range(NCORES):
        b, j = g // 4, g % 4
        e = ENDS[j]
        s = e - TP
        xc = np.ascontiguousarray(
            x[b, s:e, :].T.reshape(NCH, 128, TP).transpose(1, 0, 2))
        in_maps.append({"x": xc, "cvec": cvec, "wsrc": wsrc})

    nc = build_program()
    res = run_bass_kernel_spmd(nc, in_maps, core_ids=list(range(NCORES)))

    out = np.empty((B, T, C), dtype=np.float32)
    for g in range(NCORES):
        b, j = g // 4, g % 4
        e = ENDS[j]
        o = np.asarray(res.results[g]["out"])
        oc = o.transpose(1, 0, 2).reshape(C, TP).T
        real = REALS[j]
        out[b, e - real:e, :] = oc[TP - real:, :]
    return out
